# version 05
# baseline (speedup 1.0000x reference)
import numpy as np
import concourse.bass as bass
import concourse.mybir as mybir
from concourse.bass_utils import run_bass_kernel_spmd

F32 = mybir.dt.float32
BF16 = mybir.dt.bfloat16
I32 = mybir.dt.int32
AF = mybir.ActivationFunctionType
AL = mybir.AluOpType
AX = mybir.AxisListType
ET = mybir.EngineType

B, S, D, H, DH, DI, NL = 4, 4096, 1024, 16, 64, 4096, 4
E, FF, CAP = 8, 3584, 1280
SH = S // 2
NT = SH // 128
NTG = S // 128
NM = D // 128
NQB = S // 512
NHC = FF // 128
EXP_SHIFT = 12.0

_STAGE = "full"
_NLB = {"ln0": 1, "mid0": 1, "l0": 1, "ctx0": 1, "attn4": NL, "full": NL}[_STAGE]

_CACHE = {}


class Prog:
    ENGINES = ("sync", "gpsimd", "vector", "scalar", "tensor")

    def __init__(self):
        self.steps = {e: [] for e in self.ENGINES}
        self.ctr = {}
        self.last_wait = {}

    def inc(self, s, n):
        self.ctr[s] = self.ctr.get(s, 0) + n
        return self.ctr[s]

    def cnt(self, s):
        return self.ctr.get(s, 0)

    def on(self, engine, fn):
        self.steps[engine].append(fn)

    def play(self, engine, eng):
        for fn in self.steps[engine]:
            fn(eng)


def build_nc():
    nc = bass.Bass("TRN2", target_bir_lowering=False, debug=False)
    P = Prog()

    def din(name, shape, dt=F32):
        return nc.dram_tensor(name, list(shape), dt, kind="ExternalInput").ap()

    x_in = din("x_my", (SH, D))
    ep_in = din("epoch", (1, 16), I32)
    wqT_in = din("wqT", (NL, D, 512))
    wkT_in = din("wkT", (NL, D, 512))
    wvT_in = din("wvT", (NL, D, 512))
    woTm_in = din("woTm", (NL, 512, D))
    woTp_in = din("woTp", (NL, 512, D))
    w1T_in = din("w1T", (NL, D, DI))
    w2T_in = din("w2T", (NL, DI, D))
    bqkv_in = din("bqkv", (NL, 1536))
    u_in = din("uT", (64, NL * 8))
    b1_in = din("b1T", (128, NL * 32))
    lncols_in = din("lncols", (128, NL * 4 * NM))
    borow_in = din("borow", (1, NL * D))
    b2row_in = din("b2row", (1, NL * D))
    bvrow_in = din("bvrow", (1, NL * 512))
    trig_in = din("trig", (128, S), BF16)
    tri_in = din("tri", (128, 128), BF16)
    ltri_in = din("ltri", (128, 128), BF16)
    identb_in = din("identb", (128, 128), BF16)
    identf_in = din("identf", (128, 128), F32)
    onesb_in = din("onesb", (128, 128), BF16)
    onesf_in = din("onesf", (1, 512), F32)
    iotac_in = din("iotac", (128, CAP))
    epsi_in = din("epsiota", (128, E))
    gatewT_in = din("gatewT", (128, NM * E), BF16)
    cvals_in = din("cvals", (128, 8))
    mw1_in = din("mw1", (4, D, FF))
    mw3_in = din("mw3", (4, D, FF))
    mw2_in = din("mw2", (4, FF, D))

    y_out = nc.dram_tensor("y", [SH, D], F32, kind="ExternalOutput").ap()
    dbg_out = (nc.dram_tensor("dbg", [128, 16384], BF16, kind="ExternalOutput").ap()
               if _STAGE == "ctx0" else None)

    h_dram = nc.dram_tensor("h_dram", [SH, D], F32).ap()
    eo_dram = nc.dram_tensor("eo_dram", [4 * CAP, D], BF16).ap()

    shr_xn = [nc.dram_tensor(f"shr_xn{l}", [128, NM * S], BF16,
                             addr_space="Shared").ap() for l in range(_NLB)]
    shr_ctx = [nc.dram_tensor(f"shr_ctx{l}", [128, 2 * 4 * S], BF16,
                              addr_space="Shared").ap() for l in range(_NLB)]
    shr_m1 = nc.dram_tensor("shr_m1", [128, NTG * D], BF16, addr_space="Shared").ap()
    shr_m2 = nc.dram_tensor("shr_m2", [128, 2 * NTG * D], F32, addr_space="Shared").ap()
    shr_fl = [nc.dram_tensor(f"shr_fl{k}", [1, 32], I32, addr_space="Shared").ap()
              for k in range(10)]

    FL_M1, FL_M2 = 8, 9

    from contextlib import ExitStack
    with ExitStack() as _stk:
        AR1 = _stk.enter_context(nc.sbuf_tensor("AR1", [128, 16384], BF16))
        AR2 = _stk.enter_context(nc.sbuf_tensor("AR2", [128, 16384], BF16))
        AR3 = _stk.enter_context(nc.sbuf_tensor("AR3", [128, 16640], BF16))
        AR4 = _stk.enter_context(nc.sbuf_tensor("AR4", [128, 16384], BF16))
        AR5 = _stk.enter_context(nc.sbuf_tensor("AR5", [128, 8192], BF16))
        AR7 = _stk.enter_context(nc.sbuf_tensor("AR7", [128, 4096], BF16))
        ARF = _stk.enter_context(nc.sbuf_tensor("ARF", [128, 1024], F32))
        PTB = _stk.enter_context(nc.sbuf_tensor("PTB", [128, 1024], BF16))
        c_trig = _stk.enter_context(nc.sbuf_tensor("c_trig", [128, S], BF16))
        c_tri = _stk.enter_context(nc.sbuf_tensor("c_tri", [128, 128], BF16))
        c_ltri = _stk.enter_context(nc.sbuf_tensor("c_ltri", [128, 128], BF16))
        c_ident = _stk.enter_context(nc.sbuf_tensor("c_ident", [128, 128], BF16))
        c_identf = _stk.enter_context(nc.sbuf_tensor("c_identf", [128, 128], F32))
        c_onesb = _stk.enter_context(nc.sbuf_tensor("c_onesb", [128, 128], BF16))
        c_onesf = _stk.enter_context(nc.sbuf_tensor("c_onesf", [1, 512], F32))
        sm_bqkvrow = _stk.enter_context(nc.sbuf_tensor("sm_bqkvrow", [1, 512], F32))
        c_u = _stk.enter_context(nc.sbuf_tensor("c_u", [64, NL * 8], F32))
        c_b1 = _stk.enter_context(nc.sbuf_tensor("c_b1", [128, NL * 32], F32))
        c_ln = _stk.enter_context(nc.sbuf_tensor("c_ln", [128, NL * 4 * NM], F32))
        sm_borow = _stk.enter_context(nc.sbuf_tensor("sm_borow", [1, D], F32))
        sm_b2row = _stk.enter_context(nc.sbuf_tensor("sm_b2row", [1, D], F32))
        sm_bvrow = _stk.enter_context(nc.sbuf_tensor("sm_bvrow", [1, 512], F32))
        c_eps = _stk.enter_context(nc.sbuf_tensor("c_eps", [128, E], F32))
        c_gatewT = _stk.enter_context(nc.sbuf_tensor("c_gatewT", [128, NM * E], BF16))
        c_iotac = _stk.enter_context(nc.sbuf_tensor("c_iotac", [128, CAP], F32))
        c_cvals = _stk.enter_context(nc.sbuf_tensor("c_cvals", [128, 8], F32))
        ep_sb = _stk.enter_context(nc.sbuf_tensor("ep_sb", [1, 16], I32))
        pf_sb = _stk.enter_context(nc.sbuf_tensor("pf_sb", [1, 32], I32))
        sm_stats = _stk.enter_context(nc.sbuf_tensor("sm_stats", [128, 12], F32))
        sm_mr = _stk.enter_context(nc.sbuf_tensor("sm_mr", [128, 8], F32))
        sm_ctx = _stk.enter_context(nc.sbuf_tensor("sm_ctx", [64, 512], BF16))
        T_A = _stk.enter_context(nc.sbuf_tensor("T_A", [128, 2048], BF16))
        T_B = _stk.enter_context(nc.sbuf_tensor("T_B", [128, 2048], BF16))
        T_C = _stk.enter_context(nc.sbuf_tensor("T_C", [64, 2048], BF16))
        sm_probs = _stk.enter_context(nc.sbuf_tensor("sm_probs", [128, NTG * E], F32))
        sm_posm = _stk.enter_context(nc.sbuf_tensor("sm_posm", [128, NTG * E], F32))
        sm_wcmb = _stk.enter_context(nc.sbuf_tensor("sm_wcmb", [128, NTG * E], F32))
        sm_cin = _stk.enter_context(nc.sbuf_tensor("sm_cin", [128, NTG * E], F32))
        sm_g = _stk.enter_context(nc.sbuf_tensor("sm_g", [128, NTG * E], BF16))
        sm_gf = _stk.enter_context(nc.sbuf_tensor("sm_gf", [128, NTG * E], F32))
        sm_tot = _stk.enter_context(nc.sbuf_tensor("sm_tot", [128, 144], F32))
        ps_s0 = _stk.enter_context(nc.psum_tensor("ps_s0", [128, 1024], F32))
        ps_s1 = _stk.enter_context(nc.psum_tensor("ps_s1", [128, 1024], F32))
        ps_ctx = _stk.enter_context(nc.psum_tensor("ps_ctx", [128, 512], F32))
        ps_bc = _stk.enter_context(nc.psum_tensor("ps_bc", [128, 512], F32))
        ps_a = _stk.enter_context(nc.psum_tensor("ps_a", [128, 512], F32))
        ps_b = _stk.enter_context(nc.psum_tensor("ps_b", [128, 512], F32))
        s_io = _stk.enter_context(nc.semaphore("s_io"))
        s_w = _stk.enter_context(nc.semaphore("s_w"))
        s_p = _stk.enter_context(nc.semaphore("s_p"))
        s_t = _stk.enter_context(nc.semaphore("s_t"))
        s_s = _stk.enter_context(nc.semaphore("s_s"))
        s_v = _stk.enter_context(nc.semaphore("s_v"))
        block = _stk.enter_context(nc.Block())
        SEM = {"s_io": s_io, "s_w": s_w, "s_t": s_t, "s_s": s_s, "s_v": s_v}
        xtile = AR3.ap().bitcast(F32)[:, 0:1024]
        sm_rz = sm_bqkvrow.ap()
        PT2 = c_iotac.ap().bitcast(BF16)[:, 0:1024]
        sm_erow = sm_bqkvrow.ap()[0:1, 0:256]
        sm_mask = T_A.ap()[:, 0:128]
        sm_mask2 = T_A.ap()[:, 128:256]
        silu_sc = PTB.ap().bitcast(F32)[:, 0:512]
        sm_xt = PTB.ap()
        htile = ARF[:, 0:1024]
        htile2 = AR7.ap().bitcast(F32)[:, 0:1024]

        def wait(engine, sname, tot=None):
            if tot is None:
                tot = P.cnt(sname)
            key = (engine, sname)
            if P.last_wait.get(key, -1) >= tot:
                return
            P.last_wait[key] = tot
            sem = SEM[sname]
            P.on(engine, lambda e, sem=sem, tot=tot: e.wait_ge(sem, tot))

        def dma(eng_name, dst, src, slow=False):
            sname = "s_w" if eng_name == "gpsimd" else "s_io"
            sem = SEM[sname]
            kw = {"allow_slow_non_contiguous": True} if slow else {}

            def _go(e, d=dst, s=src, sem=sem, kw=kw):
                e.dma_start(out=d, in_=s, **kw).then_inc(sem, 16)
            P.on(eng_name, _go)
            return P.inc(sname, 16)

        def vstep(fn, n=1):
            P.on("vector", lambda e, fn=fn: fn(e).then_inc(s_v, 1))
            return P.inc("s_v", n)

        def sstep(fn, n=1):
            P.on("scalar", lambda e, fn=fn: fn(e).then_inc(s_s, 1))
            return P.inc("s_s", n)

        def tstep(fn, n=1):
            P.on("tensor", lambda e, fn=fn: fn(e).then_inc(s_t, 1))
            return P.inc("s_t", n)

        pstate = {}

        def init_parity(e):
            pid = nc.partition_id([ET.SP])
            pstate["par"] = e.compute_val(pid % 2)
            pstate["rpar"] = e.alloc_register("rpar")
            e.reg_mov(pstate["rpar"], pstate["par"])
            pstate["r_flag"] = e.alloc_register("r_flag")
            pstate["r_ep"] = e.alloc_register("r_ep")
            pstate["r_exp"] = e.alloc_register("r_exp")
            pstate["r_cond"] = e.alloc_register("r_cond")
            pstate["r_flag2"] = e.alloc_register("r_flag2")
            pstate["r_cond2"] = e.alloc_register("r_cond2")
            e.reg_mov(pstate["r_exp"], 0)

        def cpair(dst_even, dst_odd, src_even, src_odd=None):
            if src_odd is None:
                src_odd = src_even

            def _go(e, de=dst_even, do=dst_odd, se=src_even, so=src_odd):
                with e.If_eq(pstate["rpar"], 0):
                    e.dma_start(out=de, in_=se).then_inc(s_io, 16)
                with e.Else():
                    e.dma_start(out=do, in_=so).then_inc(s_io, 16)
            P.on("sync", _go)
            return P.inc("s_io", 16)

        def flag_write(event):
            wait("sync", "s_io")
            cpair(shr_fl[event][0:1, 0:16], shr_fl[event][0:1, 16:32], ep_sb[:, :])

        def flag_poll(event):
            def _go(e, ev=event):
                rf, rep, rex, rc = (pstate["r_flag"], pstate["r_ep"],
                                    pstate["r_exp"], pstate["r_cond"])
                rf2 = pstate["r_flag2"]
                rc2 = pstate["r_cond2"]
                e.reg_mov(rc, 1)
                with e.While(rc):
                    e.dma_start(out=pf_sb[0:1, 0:32],
                                in_=shr_fl[ev][0:1, 0:32]).then_inc(s_p, 16)
                    e.reg_add(rex, rex, 16)
                    e.wait_ge(s_p, rex)
                    e.load(rf, pf_sb[0:1, 0:1])
                    e.load(rf2, pf_sb[0:1, 16:17])
                    e.reg_alu(rc, rf, rep, AL.not_equal)
                    e.reg_alu(rc2, rf2, rep, AL.not_equal)
                    e.reg_alu(rc, rc, rc2, AL.add)
            P.on("sync", _go)

        for dst, src in [(c_trig[:, :], trig_in), (c_tri[:, :], tri_in),
                         (c_ltri[:, :], ltri_in), (c_ident[:, :], identb_in),
                         (c_identf[:, :], identf_in), (c_onesb[:, :], onesb_in),
                         (c_onesf[:, :], onesf_in),
                         (c_u[:, :], u_in), (c_b1[:, :], b1_in),
                         (c_ln[:, :], lncols_in),
                         (c_eps[:, :], epsi_in), (c_gatewT[:, :], gatewT_in),
                         (c_iotac[:, :], iotac_in),
                         (c_cvals[:, :], cvals_in)]:
            dma("gpsimd", dst, src)
        consts_w = P.cnt("s_w")

        CVALS = [1e-5, 1e-6, -0.5, 0.125, -float(EXP_SHIFT), -1e9, CAP + 0.5, 0.0]
        for ci, cv in enumerate(CVALS):
            nc.const_aps.register(F32, float(cv), c_cvals[:, ci:ci + 1])

        P.on("sync", init_parity)
        ep_tot = dma("sync", ep_sb[:, :], ep_in)
        P.on("sync", lambda e: (e.wait_ge(s_io, ep_tot),
                                e.load(pstate["r_ep"], ep_sb[0:1, 0:1])))
        P.last_wait[("sync", "s_io")] = ep_tot

        wait("vector", "s_w", consts_w)
        wait("scalar", "s_w", consts_w)
        wait("tensor", "s_w", consts_w)

        def ln_core(src, dst, eps):
            P.on("vector", lambda e: e.bn_stats(sm_stats[:, 0:6], src[:, 0:512]))
            vstep(lambda e: e.bn_stats(sm_stats[:, 6:12], src[:, 512:1024]))
            wait("vector", "s_v")
            vt = vstep(lambda e: e.bn_aggr(sm_mr[:, 0:2], sm_stats[:, 0:12]))
            wait("scalar", "s_v", vt)
            st1 = sstep(lambda e: e.activation(sm_mr[:, 2:3], sm_mr[:, 1:2],
                                               AF.Ln, bias=float(eps)))
            wait("scalar", "s_s", st1)
            st2 = sstep(lambda e: e.activation(sm_mr[:, 3:4], sm_mr[:, 2:3],
                                               AF.Exp, scale=-0.5))
            wait("vector", "s_s", st2)
            return vstep(lambda e, dst=dst, src=src:
                         e.tensor_scalar(dst, src, sm_mr[:, 0:1], sm_mr[:, 3:4],
                                         AL.subtract, AL.mult))

        ln0_free = [0, 0]
        ln0_hv = P.cnt("s_v")
        for st in range(NT):
            ht = htile if st % 2 == 0 else htile2
            wait("sync", "s_v", ln0_hv)
            wait("sync", "s_io", ln0_free[st % 2])
            tot = dma("sync", ht, x_in[st * 128:(st + 1) * 128, :])
            wait("vector", "s_io", tot)
            vt = ln_core(ht, ht, 1e-5)
            wait("sync", "s_v", vt)
            ln0_free[st % 2] = dma("sync", h_dram[st * 128:(st + 1) * 128, :], ht)

        def ln_pass(l, which, dstT):
            wcol0 = (l * 4 + (0 if which == 1 else 2)) * NM
            bcol0 = wcol0 + NM
            wait("sync", "s_io")
            lnht_free = [0, 0]
            lnps = {}
            last_tr = P.cnt("s_t")
            ln_hv = P.cnt("s_v")
            lnps0 = P.cnt("s_v")
            for st in range(NT):
                ht = htile if st % 2 == 0 else htile2
                wait("sync", "s_v", max(ln_hv, lnht_free[st % 2]))
                tot = dma("sync", ht, h_dram[st * 128:(st + 1) * 128, :])
                wait("vector", "s_io", tot)
                wait("vector", "s_t", last_tr)
                vt = ln_core(ht, AR5[:, 0:1024], 1e-6)
                lnht_free[st % 2] = vt
                for mt in range(NM):
                    pix = mt % 2
                    ps = ps_a if pix == 0 else ps_b
                    psv = ps.ap().bitcast(BF16)[:, 0:128]
                    wait("tensor", "s_v", max(vt, lnps.get(pix, lnps0)))
                    tt = tstep(lambda e, psv=psv, mt=mt:
                               e.transpose(psv, AR5[:, mt * 128:(mt + 1) * 128],
                                           c_ident[:, :]))
                    wait("vector", "s_t", tt)
                    wc = c_ln[:, wcol0 + mt:wcol0 + mt + 1]
                    bc = c_ln[:, bcol0 + mt:bcol0 + mt + 1]
                    dc = mt * SH + st * 128
                    lnps[pix] = vstep(
                        lambda e, psv=psv, dc=dc, wc=wc, bc=bc, dstT=dstT:
                        e.tensor_scalar(dstT[:, dc:dc + 128], psv, wc, bc,
                                        AL.mult, AL.add))
                    last_tr = P.cnt("s_t")

        def phase_qkv(l):
            wait("vector", "s_t")
            vstep(lambda e: e.memset(AR3[:, :], 1.0))
            projs = [(wqT_in, AR2, 0, "qk"), (wkT_in, AR4, 1, "qk"),
                     (wvT_in, None, 2, "v")]
            for pi, (w_in, dstT, bofs, kind) in enumerate(projs):
                wait("gpsimd", "s_t")
                wait("gpsimd", "s_v")
                wtot = dma("gpsimd",
                           AR7.ap().rearrange("p (mt d) -> p mt d", mt=NM),
                           w_in[l].rearrange("(mt p) d -> p mt d", p=128))
                wait("tensor", "s_w", wtot)
                if kind == "qk":
                    wait("sync", "s_t")
                    brtot = dma("sync", sm_bqkvrow[0:1, :],
                                bqkv_in[l:l + 1, bofs * 512:(bofs + 1) * 512])
                    wait("tensor", "s_io", brtot)
                if pi == 0:
                    c0 = P.cnt("s_t")
                    qkv_state = {"chunk": [c0, c0], "pse": {}, "psv": {},
                                 "pse0": P.cnt("s_s"), "psv0": P.cnt("s_v")}
                st_ = qkv_state
                for scg in range(8):
                    if pi == 0 and scg == 0:
                        wait("sync", "s_io")
                        flag_poll(2 * l)
                    buf = scg % 2
                    bufc = buf * 4096
                    wait("sync", "s_t", st_["chunk"][buf])
                    sxn = shr_xn[l].rearrange("p (mt s) -> p mt s", mt=NM)
                    ctot = dma("sync",
                               AR5.ap().rearrange("p (b mt s) -> p b mt s",
                                                  b=2, mt=NM)[:, buf],
                               sxn[:, :, scg * 512:(scg + 1) * 512])
                    wait("tensor", "s_io", ctot)
                    if kind == "qk":
                        for dch in range(4):
                            pix = (scg * 4 + dch) % 2
                            ps = ps_s0 if pix == 0 else ps_s1
                            psr = ps.ap()[:, 0:512]
                            wait("tensor", "s_s", st_["pse"].get(pix, st_["pse0"]))

                            brow = dch * 128

                            def _mm(e, psr=psr, dch=dch, bufc=bufc, brow=brow):
                                for mt in range(NM):
                                    e.matmul(
                                        psr,
                                        AR7[:, mt * 512 + dch * 128:
                                            mt * 512 + dch * 128 + 128],
                                        AR5[:, bufc + mt * 512:bufc + mt * 512 + 512],
                                        start=(mt == 0), stop=False)
                                return e.matmul(psr,
                                                sm_bqkvrow[0:1, brow:brow + 128],
                                                c_onesf[0:1, 0:512],
                                                start=False, stop=True)
                            tt = tstep(_mm)
                            wait("scalar", "s_t", tt)
                            dc = dch * 4096 + scg * 512
                            st_["pse"][pix] = sstep(
                                lambda e, psr=psr, dc=dc, dstT=dstT:
                                e.copy(dstT[:, dc:dc + 512], psr))
                            st_["chunk"][buf] = P.cnt("s_t")
                    else:
                        for sv in range(4):
                            stg = scg * 4 + sv
                            pix = stg % 2
                            ps = ps_s0 if pix == 0 else ps_s1
                            psr = ps.ap()[:, 0:512]
                            wait("tensor", "s_v", st_["psv"].get(pix, st_["psv0"]))

                            def _mm(e, psr=psr, sv=sv, bufc=bufc):
                                for mt in range(NM):
                                    e.matmul(psr,
                                             AR5[:, bufc + mt * 512 + sv * 128:
                                                 bufc + mt * 512 + sv * 128 + 128],
                                             AR7[:, mt * 512:mt * 512 + 512],
                                             start=(mt == 0), stop=False)
                                return e.matmul(psr, c_onesf[0:1, 0:128],
                                                sm_bvrow[0:1, :],
                                                start=False, stop=True)
                            tt = tstep(_mm)
                            wait("vector", "s_t", tt)
                            dstv = AR3.ap().rearrange("p (h k c) -> p h k c",
                                                      h=8, k=32)[:, :, stg, 0:64]
                            srcv = psr.rearrange("p (h c) -> p h c", h=8)
                            st_["psv"][pix] = vstep(
                                lambda e, dstv=dstv, srcv=srcv:
                                e.tensor_copy(dstv, srcv))
                            st_["chunk"][buf] = P.cnt("s_t")

        def phase_attn(l):
            wait("vector", "s_t")
            vstep(lambda e: e.tensor_copy(AR5[64:128, 4096:8192], c_trig[64:128, :]))
            for h in range(8):
                hrow = 64 * (h % 2)
                hcol = (h // 2) * 4096
                wait("vector", "s_t")
                vstep(lambda e, hrow=hrow, hcol=hcol:
                      e.tensor_copy(AR5[0:64, 0:4096],
                                    AR2[hrow:hrow + 64, hcol:hcol + 4096]))
                vstep(lambda e, hrow=hrow, hcol=hcol:
                      e.tensor_copy(AR5[0:64, 4096:8192],
                                    AR4[hrow:hrow + 64, hcol:hcol + 4096]))
                ucol = l * 8 + h
                for hp in range(2):
                    cc = hp * 2048
                    vstep(lambda e, cc=cc, ucol=ucol:
                          e.tensor_scalar(T_A[0:32, :], AR5[0:32, cc:cc + 2048],
                                          c_u[0:32, ucol:ucol + 1], None, AL.add))
                    vstep(lambda e, cc=cc, ucol=ucol:
                          e.tensor_scalar(T_A[32:64, :], AR5[32:64, cc:cc + 2048],
                                          c_u[32:64, ucol:ucol + 1], None, AL.add))
                    wait("vector", "s_v")
                    vstep(lambda e: e.tensor_copy(T_B[32:64, :], T_A[0:32, :]))
                    vstep(lambda e: e.tensor_copy(T_B[0:32, :], T_A[32:64, :]))
                    wait("vector", "s_v")
                    vstep(lambda e, cc=cc: e.tensor_mul(T_A[64:96, :], T_A[0:32, :],
                                                        c_trig[0:32, cc:cc + 2048]))
                    vstep(lambda e, cc=cc: e.tensor_mul(T_B[64:96, :], T_A[32:64, :],
                                                        c_trig[32:64, cc:cc + 2048]))
                    wait("vector", "s_v")
                    vstep(lambda e: e.tensor_add(T_C[0:32, :], T_A[64:96, :],
                                                 T_B[64:96, :]))
                    wait("vector", "s_v")
                    vstep(lambda e, cc=cc: e.tensor_mul(T_A[64:96, :], T_B[0:32, :],
                                                        c_trig[0:32, cc:cc + 2048]))
                    vstep(lambda e, cc=cc: e.tensor_mul(T_B[64:96, :], T_B[32:64, :],
                                                        c_trig[32:64, cc:cc + 2048]))
                    wait("vector", "s_v")
                    vstep(lambda e: e.tensor_sub(T_C[32:64, :], T_A[64:96, :],
                                                 T_B[64:96, :]))
                    wait("vector", "s_v")
                    vstep(lambda e, cc=cc:
                          e.tensor_copy(AR5[64:128, cc:cc + 2048], T_C[:, :]))
                asm_v = P.cnt("s_v")
                exp_done = [P.cnt("s_s")] * 2
                pv_done = [P.cnt("s_t")] * 2

                for qb in range(NQB):
                    nkt = 4 * qb + 4
                    npp = (nkt + 1) // 2
                    pend_pv = None

                    def mk_pv(kts, qb=qb, h=h, nkt=nkt, pt=None):
                        def _pv(e, kts=kts, qb=qb, h=h, nkt=nkt, pt=pt):
                            last = None
                            for j, kt in enumerate(kts):
                                dlt = max(0, (kt - 4 * qb) * 128)
                                vofs = (h * 32 + kt) * 65
                                last = e.matmul(
                                    ps_ctx[0:65, dlt:512],
                                    AR3[:, vofs:vofs + 65],
                                    pt[:, j * 512 + dlt:(j + 1) * 512],
                                    start=(kt == 0), stop=(kt == nkt - 1))
                            return last
                        return _pv

                    for pp in range(npp):
                        kts = [2 * pp + j for j in (0, 1) if 2 * pp + j < nkt]
                        ps = ps_s0 if pp % 2 == 0 else ps_s1
                        pt = PTB.ap() if pp % 2 == 0 else PT2
                        wait("tensor", "s_s", exp_done[pp % 2])
                        wait("tensor", "s_v", asm_v)

                        def _sc(e, kts=kts, qb=qb, ps=ps):
                            last = None
                            for j, kt in enumerate(kts):
                                dlt = max(0, (kt - 4 * qb) * 128)
                                last = e.matmul(
                                    ps.ap()[:, j * 512 + dlt:(j + 1) * 512],
                                    AR5[:, 4096 + kt * 128:4096 + kt * 128 + 128],
                                    AR5[:, qb * 512 + dlt:qb * 512 + 512],
                                    start=True, stop=True)
                            return last
                        tt = tstep(_sc)
                        if pend_pv is not None:
                            tstep(pend_pv[0])
                            pv_done[pend_pv[1]] = P.cnt("s_t")
                        wait("scalar", "s_t", max(tt, pv_done[pp % 2]))

                        def _exp(e, kts=kts, qb=qb, ps=ps, pt=pt):
                            last = None
                            for j, kt in enumerate(kts):
                                dlt = max(0, (kt - 4 * qb) * 128)
                                last = e.activation(
                                    pt[:, j * 512 + dlt:(j + 1) * 512],
                                    ps.ap()[:, j * 512 + dlt:(j + 1) * 512],
                                    AF.Exp, bias=-EXP_SHIFT, scale=0.125)
                            return last
                        et = sstep(_exp)
                        exp_done[pp % 2] = et
                        need_v = any(kt >= 4 * qb for kt in kts)
                        if need_v:
                            wait("vector", "s_s", et)

                            def _msk(e, kts=kts, qb=qb, pt=pt):
                                last = None
                                for j, kt in enumerate(kts):
                                    if kt >= 4 * qb:
                                        dlt = (kt - 4 * qb) * 128
                                        last = e.tensor_mul(
                                            pt[:, j * 512 + dlt:j * 512 + dlt + 128],
                                            pt[:, j * 512 + dlt:j * 512 + dlt + 128],
                                            c_tri[:, :])
                                return last
                            mt_ = vstep(_msk)
                            wait("tensor", "s_v", mt_)
                        wait("tensor", "s_s", et)
                        pend_pv = (mk_pv(kts, pt=pt), pp % 2)
                    tstep(pend_pv[0])
                    pv_done[pend_pv[1]] = P.cnt("s_t")
                    wait("vector", "s_t")
                    rv = vstep(lambda e: e.reciprocal(sm_rz[0:1, :],
                                                      ps_ctx[64:65, :]))
                    wait("tensor", "s_v", rv)
                    bt_ = tstep(lambda e: e.matmul(ps_bc[0:64, :],
                                                   c_onesf[0:1, 0:64],
                                                   sm_rz[0:1, :],
                                                   start=True, stop=True))
                    wait("scalar", "s_t", bt_)
                    ct_ = sstep(lambda e: e.copy(sm_ctx[:, :], ps_ctx[0:64, :]))
                    wait("vector", "s_t", bt_)
                    wait("vector", "s_s", ct_)
                    dc = hcol + qb * 512
                    vstep(lambda e, hrow=hrow, dc=dc:
                          e.tensor_mul(AR1[hrow:hrow + 64, dc:dc + 512],
                                       sm_ctx[:, :], ps_bc[0:64, :]))
                    wait("tensor", "s_v")

        def phase_wo(l):
            a1 = AR1.ap().rearrange("p (d s) -> p d s", d=4)
            sc = shr_ctx[l].rearrange("p (o d s) -> p o d s", o=2, d=4)
            wait("sync", "s_v")
            cpair(sc[:, 0], sc[:, 1], a1)
            flag_write(2 * l + 1)
            wait("sync", "s_t")
            wait("sync", "s_io")
            a4 = AR4.ap().rearrange("p (d s) -> p d s", d=8)[:, 0:4]
            cpair(a4, a4, sc[:, 0, :, 0:SH], sc[:, 1, :, SH:S])
            wait("gpsimd", "s_v")
            w_m = dma("gpsimd",
                      AR2.ap()[:, 0:4096].rearrange("p (d m) -> p d m", d=4),
                      woTm_in[l].rearrange("(d p) m -> p d m", p=128))
            w_p = dma("gpsimd",
                      AR2.ap()[:, 8192:12288].rearrange("p (d m) -> p d m", d=4),
                      woTp_in[l].rearrange("(d p) m -> p d m", p=128))
            flag_poll(2 * l + 1)
            wait("sync", "s_t")
            a5 = AR5.ap().rearrange("p (d s) -> p d s", d=4)
            ptot = cpair(a5, a5, sc[:, 1, :, 0:SH], sc[:, 0, :, SH:S])
            wait("tensor", "s_io", ptot)
            wait("tensor", "s_w", w_p)
            psl4 = [ps_a, ps_b, ps_ctx, ps_bc]
            wo_pse = {}
            wo_pse0 = P.cnt("s_v")
            ht_free = [0, 0]
            wo_hv = P.cnt("s_v")
            for st in range(NT):
                ht = htile if st % 2 == 0 else htile2
                wait("sync", "s_v", wo_hv)
                wait("sync", "s_io", ht_free[st % 2])
                htot = dma("sync", ht, h_dram[st * 128:(st + 1) * 128, :])
                addv = 0
                for mh in range(2):
                    bidx = (st * 2 + mh) % 4
                    ps = psl4[bidx]
                    wait("tensor", "s_v", wo_pse.get(bidx, wo_pse0))

                    def _mm(e, st=st, mh=mh, ps=ps, l=l):
                        for dt_ in range(4):
                            e.matmul(ps.ap()[:, :],
                                     AR4[:, dt_ * 2048 + st * 128:
                                         dt_ * 2048 + st * 128 + 128],
                                     AR2[:, dt_ * 1024 + mh * 512:
                                         dt_ * 1024 + mh * 512 + 512],
                                     start=(dt_ == 0), stop=False)
                        for dt_ in range(4):
                            e.matmul(ps.ap()[:, :],
                                     AR5[:, dt_ * 2048 + st * 128:
                                         dt_ * 2048 + st * 128 + 128],
                                     AR2[:, 8192 + dt_ * 1024 + mh * 512:
                                         8192 + dt_ * 1024 + mh * 512 + 512],
                                     start=False, stop=False)
                        return e.matmul(ps.ap()[:, :], c_onesf[0:1, 0:128],
                                        sm_borow[0:1, mh * 512:mh * 512 + 512],
                                        start=False, stop=True)
                    tt = tstep(_mm)
                    wait("vector", "s_t", tt)
                    wait("vector", "s_io", htot)
                    addv = vstep(lambda e, mh=mh, ps=ps, ht=ht:
                                 e.tensor_add(ht[:, mh * 512:(mh + 1) * 512],
                                              ht[:, mh * 512:(mh + 1) * 512],
                                              ps.ap()[:, :]))
                    wo_pse[bidx] = addv
                wait("sync", "s_v", addv)
                ht_free[st % 2] = dma("sync", h_dram[st * 128:(st + 1) * 128, :], ht)

        def phase_ffn(l):
            wait("vector", "s_t")
            wait("scalar", "s_t")
            wait("scalar", "s_io")
            ln_pass(l, 2, AR4)
            wait("gpsimd", "s_t")
            w1c = []
            w1v = w1T_in[l].rearrange("(mt p) (hc x) -> p hc mt x", p=128, x=128)
            for hc in range(32):
                arr = AR2 if hc < 16 else AR3
                base = (hc % 16) * 1024
                w1c.append(dma("gpsimd",
                               arr.ap()[:, base:base + 1024].rearrange(
                                   "p (mt x) -> p mt x", x=128),
                               w1v[:, hc]))
            ffn_pse = {}
            ffn_ffv = {}
            ffn_pse0 = P.cnt("s_s")
            fnT_v = P.cnt("s_v")
            wait("tensor", "s_v", fnT_v)
            for sc_ in range(4):
                for hc in range(32):
                    pix = hc % 2
                    ps = ps_s0 if pix == 0 else ps_s1
                    psr = ps.ap()[:, 0:512]
                    arr = AR2 if hc < 16 else AR3
                    base = (hc % 16) * 1024
                    wait("tensor", "s_w", w1c[hc])
                    wait("tensor", "s_s")
                    wait("tensor", "s_v")

                    def _mm(e, psr=psr, arr=arr, base=base, sc_=sc_):
                        last = None
                        for mt in range(NM):
                            last = e.matmul(
                                psr, arr[:, base + mt * 128:base + mt * 128 + 128],
                                AR4[:, mt * SH + sc_ * 512:mt * SH + sc_ * 512 + 512],
                                start=(mt == 0), stop=(mt == NM - 1))
                        return last
                    tt = tstep(_mm)
                    wait("scalar", "s_t", tt)
                    bcol = c_b1[:, l * 32 + hc:l * 32 + hc + 1]
                    ffn_pse[pix] = sstep(
                        lambda e, psr=psr, hc=hc, bcol=bcol:
                        e.activation(AR1[:, hc * 512:hc * 512 + 512], psr,
                                     AF.Gelu, bias=bcol))
                psl = [ps_s0.ap()[:, 0:512], ps_s0.ap()[:, 512:1024],
                       ps_s1.ap()[:, 0:512], ps_s1.ap()[:, 512:1024],
                       ps_ctx.ap()[:, :], ps_bc.ap()[:, :],
                       ps_a.ap()[:, :], ps_b.ap()[:, :]]
                w2v = w2T_in[l].rearrange("(hg p) m -> p hg m", p=128)
                wait("tensor", "s_s")
                wait("tensor", "s_v")
                w2buf = getattr(phase_ffn, "_w2buf", [0, 0])
                wait("gpsimd", "s_v", fnT_v)
                for hg in range(8):
                    buf = hg % 2
                    wait("gpsimd", "s_t", w2buf[buf])
                    wtot = dma("gpsimd",
                               AR5.ap().rearrange("p (b c m) -> p b c m",
                                                  b=2, c=4)[:, buf],
                               w2v[:, hg * 4:(hg + 1) * 4])
                    wait("tensor", "s_w", wtot)

                    def _mm(e, hg=hg, buf=buf, sc_=sc_):
                        last = None
                        for hx in range(4):
                            ht = hg * 4 + hx
                            for stc in range(4):
                                for mh in range(2):
                                    last = e.matmul(
                                        psl[stc * 2 + mh],
                                        AR1[:, ht * 512 + stc * 128:
                                            ht * 512 + stc * 128 + 128],
                                        AR5[:, buf * 4096 + hx * 1024 + mh * 512:
                                            buf * 4096 + hx * 1024 + mh * 512 + 512],
                                        start=(ht == 0), stop=False)
                        return last
                    tstep(_mm)
                    w2buf[buf] = P.cnt("s_t")
                phase_ffn._w2buf = w2buf

                def _mmb(e, l=l):
                    last = None
                    for stc in range(4):
                        for mh in range(2):
                            last = e.matmul(psl[stc * 2 + mh],
                                            c_onesf[0:1, 0:128],
                                            sm_b2row[0:1, mh * 512:mh * 512 + 512],
                                            start=False, stop=True)
                    return last
                tt = tstep(_mmb)
                ffht = getattr(phase_ffn, "_ffht", [0, 0])
                for stc in range(4):
                    st = sc_ * 4 + stc
                    ht = htile if stc % 2 == 0 else htile2
                    wait("sync", "s_v", fnT_v)
                    wait("sync", "s_io", ffht[stc % 2])
                    htot = dma("sync", ht, h_dram[st * 128:(st + 1) * 128, :])
                    wait("vector", "s_t", tt)
                    wait("vector", "s_io", htot)
                    av = 0
                    for mh in range(2):
                        av = vstep(lambda e, stc=stc, mh=mh, ht=ht:
                                   e.tensor_add(ht[:, mh * 512:(mh + 1) * 512],
                                                ht[:, mh * 512:(mh + 1) * 512],
                                                psl[stc * 2 + mh]))
                        ffn_ffv[stc * 2 + mh] = av
                    wait("sync", "s_v", av)
                    ffht[stc % 2] = dma("sync",
                                        h_dram[st * 128:(st + 1) * 128, :], ht)
                phase_ffn._ffht = ffht

        def emit_moe():
            wait("gpsimd", "s_t")
            wait("gpsimd", "s_s")
            iot = dma("gpsimd", c_iotac[:, :], iotac_in)
            wait("vector", "s_w", iot)
            wait("sync", "s_io")
            m0_cp = 0
            m0_free = [P.cnt("s_v")] * 2
            m0_x = [P.cnt("s_v")] * 2
            xt2 = AR3.ap().bitcast(F32)[:, 1024:2048]
            m0_hv = P.cnt("s_v")
            for st in range(NT):
                ht = htile if st % 2 == 0 else htile2
                xt = xtile if st % 2 == 0 else xt2
                wait("sync", "s_v", max(m0_hv, m0_free[st % 2]))
                t1 = dma("sync", ht, h_dram[st * 128:(st + 1) * 128, :])
                wait("sync", "s_v", m0_x[st % 2])
                t2 = dma("sync", xt, x_in[st * 128:(st + 1) * 128, :])
                wait("vector", "s_io", t2)
                m0_x[st % 2] = vstep(lambda e, ht=ht, xt=xt:
                                     e.tensor_add(ht, ht, xt))
                wait("vector", "s_v")
                wait("vector", "s_io", m0_cp)
                vt = ln_core(ht, PTB[:, 0:1024], 1e-5)
                m0_free[st % 2] = vt
                wait("sync", "s_v", vt)
                dma("sync", h_dram[st * 128:(st + 1) * 128, :], ht)
                m0_cp = cpair(shr_m1[:, st * D:(st + 1) * D],
                              shr_m1[:, (NT + st) * D:(NT + st + 1) * D],
                              PTB[:, 0:1024])
            flag_write(FL_M1)
            flag_poll(FL_M1)
            wait("sync", "s_t")
            wait("sync", "s_s")
            xn1tot = []
            for stg in range(NTG):
                arr = AR1 if stg < 16 else AR2
                xn1tot.append(dma("sync", arr[:, (stg % 16) * 1024:
                                              (stg % 16) * 1024 + 1024],
                                  shr_m1[:, stg * D:(stg + 1) * D]))

            def xn1t(stg):
                arr = AR1 if stg < 16 else AR2
                return arr[:, (stg % 16) * 1024:(stg % 16) * 1024 + 1024]

            for stg in range(NTG):
                wait("tensor", "s_io", xn1tot[stg])
                for mt in range(NM):
                    ps = ps_a if mt % 2 == 0 else ps_b
                    psv = ps.ap().bitcast(BF16)[:, 0:128]
                    wait("tensor", "s_v")
                    tt = tstep(lambda e, psv=psv, mt=mt, stg=stg:
                               e.transpose(psv, xn1t(stg)[:, mt * 128:(mt + 1) * 128],
                                           c_ident[:, :]))
                    wait("vector", "s_t", tt)
                    vstep(lambda e, psv=psv, mt=mt:
                          e.tensor_copy(sm_xt[:, mt * 128:(mt + 1) * 128], psv))
                wait("tensor", "s_v")

                def _rmm(e):
                    last = None
                    for mt in range(NM):
                        last = e.matmul(ps_ctx.ap()[:, 0:8],
                                        sm_xt[:, mt * 128:(mt + 1) * 128],
                                        c_gatewT[:, mt * 8:(mt + 1) * 8],
                                        start=(mt == 0), stop=(mt == NM - 1))
                    return last
                tt = tstep(_rmm)
                wait("vector", "s_t", tt)
                vstep(lambda e: e.reduce_max(sm_mr[:, 4:5], ps_ctx.ap()[:, 0:8],
                                             axis=AX.X, negate=True))
                wait("scalar", "s_v")
                sstep(lambda e: e.activation(sm_tot[:, 0:8], ps_ctx.ap()[:, 0:8],
                                             AF.Exp, bias=sm_mr[:, 4:5]))
                wait("vector", "s_s")
                vstep(lambda e: e.reduce_sum(sm_mr[:, 5:6], sm_tot[:, 0:8], axis=AX.X))
                wait("vector", "s_v")
                vstep(lambda e: e.reciprocal(sm_mr[:, 6:7], sm_mr[:, 5:6]))
                wait("vector", "s_v")
                vstep(lambda e, stg=stg:
                      e.tensor_scalar(sm_probs[:, stg * 8:stg * 8 + 8],
                                      sm_tot[:, 0:8], sm_mr[:, 6:7], None, AL.mult))
                wait("vector", "s_v")
                vstep(lambda e, stg=stg:
                      e.tensor_sub(sm_tot[:, 8:16], sm_probs[:, stg * 8:stg * 8 + 8],
                                   c_eps[:, :]))
                wait("vector", "s_v")
                vstep(lambda e: e.reduce_max(sm_mr[:, 7:8], sm_tot[:, 8:16], axis=AX.X))
                wait("vector", "s_v")
                vstep(lambda e: e.tensor_scalar(sm_tot[:, 16:24], sm_tot[:, 8:16],
                                                sm_mr[:, 7:8], None, AL.is_ge))
                wait("vector", "s_v")
                vstep(lambda e: e.scalar_tensor_tensor(sm_tot[:, 24:32],
                                                       sm_tot[:, 16:24], -1e9,
                                                       sm_tot[:, 8:16],
                                                       AL.mult, AL.add))
                wait("vector", "s_v")
                vstep(lambda e: e.reduce_max(sm_mr[:, 7:8], sm_tot[:, 24:32],
                                             axis=AX.X))
                wait("vector", "s_v")
                vstep(lambda e: e.tensor_scalar(sm_tot[:, 32:40], sm_tot[:, 24:32],
                                                sm_mr[:, 7:8], None, AL.is_ge))
                wait("vector", "s_v")
                vstep(lambda e, stg=stg:
                      e.tensor_add(sm_g[:, stg * 8:stg * 8 + 8], sm_tot[:, 16:24],
                                   sm_tot[:, 32:40]))
                vstep(lambda e, stg=stg:
                      e.tensor_add(sm_gf[:, stg * 8:stg * 8 + 8], sm_tot[:, 16:24],
                                   sm_tot[:, 32:40]))
                wait("tensor", "s_v")
                tstep(lambda e, stg=stg:
                      e.matmul(ps_b.ap()[:, 0:8], c_ltri[:, :],
                               sm_g[:, stg * 8:stg * 8 + 8], start=True, stop=True))
                tt = tstep(lambda e, stg=stg:
                           e.matmul(ps_bc[0:8, stg:stg + 1],
                                    sm_g[:, stg * 8:stg * 8 + 8],
                                    c_onesb[:, 0:1], start=True, stop=True))
                wait("vector", "s_t", tt)
                vstep(lambda e, stg=stg:
                      e.tensor_copy(sm_cin[:, stg * 8:stg * 8 + 8],
                                    ps_b.ap()[:, 0:8]))
                wait("tensor", "s_v")

            wait("vector", "s_t")
            vstep(lambda e: e.tensor_copy(sm_tot[0:8, 0:32], ps_bc[0:8, 0:32]))
            wait("vector", "s_v")
            vstep(lambda e: e.tensor_copy(sm_tot[0:8, 80:112], sm_tot[0:8, 0:32]))
            cur, oth = 0, 40
            for k in (1, 2, 4, 8, 16):
                wait("vector", "s_v")
                vstep(lambda e, cur=cur, oth=oth:
                      e.tensor_copy(sm_tot[0:8, oth:oth + 32],
                                    sm_tot[0:8, cur:cur + 32]))
                wait("vector", "s_v")
                vstep(lambda e, cur=cur, oth=oth, k=k:
                      e.tensor_add(sm_tot[0:8, oth + k:oth + 32],
                                   sm_tot[0:8, cur + k:cur + 32],
                                   sm_tot[0:8, cur:cur + 32 - k]))
                cur, oth = oth, cur
            wait("vector", "s_v")
            vstep(lambda e, cur=cur:
                  e.tensor_sub(sm_tot[0:8, 112:144], sm_tot[0:8, cur:cur + 32],
                               sm_tot[0:8, 80:112]))
            exv = P.cnt("s_v")
            wait("sync", "s_v", exv)
            erview = sm_erow.rearrange("o (t e) -> o t e", e=8)
            for e_ in range(8):
                dma("sync", erview[:, :, e_], sm_tot[e_:e_ + 1, 112:144], slow=True)
            er_tot = P.cnt("s_io")

            for stg in range(NTG):
                wait("tensor", "s_io", er_tot)
                wait("tensor", "s_v")
                tt = tstep(lambda e, stg=stg:
                           e.matmul(ps_a.ap()[:, 0:8], c_onesf[0:1, 0:128],
                                    sm_erow[0:1, stg * 8:stg * 8 + 8],
                                    start=True, stop=True))
                wait("vector", "s_t", tt)
                vstep(lambda e, stg=stg:
                      e.tensor_add(sm_tot[:, 0:8], sm_cin[:, stg * 8:stg * 8 + 8],
                                   ps_a.ap()[:, 0:8]))
                wait("vector", "s_v")
                vstep(lambda e, stg=stg:
                      e.scalar_tensor_tensor(sm_tot[:, 8:16], sm_tot[:, 0:8],
                                             CAP + 0.5, sm_gf[:, stg * 8:stg * 8 + 8],
                                             AL.is_le, AL.mult))
                wait("vector", "s_v")
                vstep(lambda e, stg=stg:
                      e.tensor_mul(sm_wcmb[:, stg * 8:stg * 8 + 8], sm_tot[:, 8:16],
                                   sm_probs[:, stg * 8:stg * 8 + 8]))
                vstep(lambda e, stg=stg:
                      e.tensor_mul(sm_posm[:, stg * 8:stg * 8 + 8], sm_tot[:, 0:8],
                                   sm_tot[:, 8:16]))
                wait("vector", "s_v")

            CCH = [(0, 512), (512, 512), (1024, 256)]
            for ex in range(4):
                for cb, cw in CCH:
                    ncp = cw // 128
                    for cp in range(ncp):
                        cptg = (cb // 128) + cp
                        wait("tensor", "s_v")
                        mk_mm = [P.cnt("s_t")] * 2
                        mk_bld = [P.cnt("s_v")] * 2
                        for st in range(NTG):
                            mk = sm_mask if st % 2 == 0 else sm_mask2
                            wait("vector", "s_t", mk_mm[st % 2])
                            mk_bld[st % 2] = vstep(
                                lambda e, mk=mk, cptg=cptg, st=st, ex=ex:
                                e.tensor_scalar(mk[:, :],
                                                c_iotac[:, cptg * 128:
                                                        (cptg + 1) * 128],
                                                sm_posm[:, st * 8 + ex:
                                                        st * 8 + ex + 1],
                                                None, AL.is_equal))
                            wait("tensor", "s_v", mk_bld[st % 2])

                            def _mm(e, mk=mk, st=st, first=(st == 0),
                                    last=(st == NTG - 1)):
                                e.matmul(ps_a.ap()[:, :], mk[:, :],
                                         xn1t(st)[:, 0:512],
                                         start=first, stop=last)
                                return e.matmul(ps_b.ap()[:, :], mk[:, :],
                                                xn1t(st)[:, 512:1024],
                                                start=first, stop=last)
                            mk_mm[st % 2] = tstep(_mm)
                        wait("scalar", "s_t")
                        sstep(lambda e, cp=cp:
                              e.copy(AR3[:, cp * 1024:cp * 1024 + 512],
                                     ps_a.ap()[:, :]))
                        sstep(lambda e, cp=cp:
                              e.copy(AR3[:, cp * 1024 + 512:cp * 1024 + 1024],
                                     ps_b.ap()[:, :]))
                        wait("tensor", "s_s")
                    for cp in range(ncp):
                        for mt in range(NM):
                            ps = ps_a if (cp * NM + mt) % 2 == 0 else ps_b
                            psv = ps.ap().bitcast(BF16)[:, 0:128]
                            wait("tensor", "s_v")
                            tt = tstep(lambda e, psv=psv, cp=cp, mt=mt:
                                       e.transpose(psv,
                                                   AR3[cp * 128:(cp + 1) * 128,
                                                       0:0]
                                                   if False else
                                                   AR3[:, cp * 1024 + mt * 128:
                                                       cp * 1024 + mt * 128 + 128],
                                                   c_ident[:, :]))
                            wait("vector", "s_t", tt)
                            vstep(lambda e, psv=psv, cp=cp, mt=mt:
                                  e.tensor_copy(AR3[:, 4096 + mt * 512 + cp * 128:
                                                    4096 + mt * 512 + cp * 128 + 128],
                                                psv))
                    wait("vector", "s_t")
                    w1v = mw1_in[ex].rearrange("(mt p) (hg x) -> p hg mt x",
                                               p=128, x=128)
                    w3v = mw3_in[ex].rearrange("(mt p) (hg x) -> p hg mt x",
                                               p=128, x=128)
                    for hg in range(14):
                        wait("gpsimd", "s_t")
                        wait("gpsimd", "s_v")
                        for hx in range(2):
                            dma("gpsimd",
                                AR7.ap()[:, hx * 1024:(hx + 1) * 1024].rearrange(
                                    "p (mt x) -> p mt x", x=128),
                                w1v[:, hg * 2 + hx])
                            dma("gpsimd",
                                AR7.ap()[:, 2048 + hx * 1024:2048 + (hx + 1) * 1024]
                                .rearrange("p (mt x) -> p mt x", x=128),
                                w3v[:, hg * 2 + hx])
                        wt3 = P.cnt("s_w")
                        wait("tensor", "s_w", wt3)
                        for hx in range(2):
                            hc = hg * 2 + hx
                            wait("tensor", "s_v")

                            def _mm(e, hx=hx, cw=cw):
                                last = None
                                for mt in range(NM):
                                    e.matmul(ps_s0.ap()[:, 0:cw],
                                             AR7[:, hx * 1024 + mt * 128:
                                                 hx * 1024 + mt * 128 + 128],
                                             AR3[:, 4096 + mt * 512:
                                                 4096 + mt * 512 + cw],
                                             start=(mt == 0), stop=(mt == NM - 1))
                                for mt in range(NM):
                                    last = e.matmul(
                                        ps_s1.ap()[:, 0:cw],
                                        AR7[:, 2048 + hx * 1024 + mt * 128:
                                            2048 + hx * 1024 + mt * 128 + 128],
                                        AR3[:, 4096 + mt * 512:
                                            4096 + mt * 512 + cw],
                                        start=(mt == 0), stop=(mt == NM - 1))
                                return last
                            tt = tstep(_mm)
                            wait("scalar", "s_t", tt)
                            st_ = sstep(lambda e, cw=cw:
                                        e.activation(silu_sc[:, 0:cw],
                                                     ps_s0.ap()[:, 0:cw], AF.Silu))
                            wait("vector", "s_s", st_)
                            vstep(lambda e, hc=hc, cw=cw:
                                  e.tensor_mul(AR4[:, hc * 512:hc * 512 + cw],
                                               silu_sc[:, 0:cw],
                                               ps_s1.ap()[:, 0:cw]))
                    wait("tensor", "s_v")
                    w2v = mw2_in[ex].rearrange("(hg p) m -> p hg m", p=128)
                    for hg in range(7):
                        buf = hg % 2
                        wait("gpsimd", "s_t")
                        wtot = dma("gpsimd",
                                   AR5.ap().rearrange("p (b c m) -> p b c m",
                                                      b=2, c=4)[:, buf],
                                   w2v[:, hg * 4:(hg + 1) * 4])
                        wait("tensor", "s_w", wtot)

                        def _mm(e, hg=hg, buf=buf, ncp=ncp, cw=cw):
                            last = None
                            for hx in range(4):
                                hc = hg * 4 + hx
                                for cp in range(ncp):
                                    cpw = min(128, cw - cp * 128)
                                    for mh in range(2):
                                        last = e.matmul(
                                            psl8[cp * 2 + mh],
                                            AR4[:, hc * 512 + cp * 128:
                                                hc * 512 + cp * 128 + cpw],
                                            AR5[:, buf * 4096 + hx * 1024 +
                                                mh * 512:buf * 4096 + hx * 1024 +
                                                mh * 512 + 512],
                                            start=(hc == 0), stop=(hc == NHC - 1))
                            return last
                        tstep(_mm)
                    wait("scalar", "s_t")
                    for cp in range(ncp):
                        for mh in range(2):
                            sstep(lambda e, cp=cp, mh=mh:
                                  e.copy(AR3[:, 8192 + cp * 1024 + mh * 512:
                                             8192 + cp * 1024 + mh * 512 + 512],
                                         psl8[cp * 2 + mh]))
                    stot = P.cnt("s_s")
                    wait("sync", "s_s", stot)
                    for cp in range(ncp):
                        cptg = (cb // 128) + cp
                        dma("sync",
                            eo_dram[ex * CAP + cptg * 128:
                                    ex * CAP + cptg * 128 + 128, :],
                            AR3[:, 8192 + cp * 1024:8192 + cp * 1024 + 1024])
                    wait("tensor", "s_io")

            wait("sync", "s_t")
            eot = []
            for k in range(40):
                arr, base = ((AR1, k) if k < 16 else
                             (AR2, k - 16) if k < 32 else (AR5, k - 32))
                eot.append(dma("sync", arr[:, base * 1024:(base + 1) * 1024],
                               eo_dram[k * 128:(k + 1) * 128, :]))

            def eo_tile(k):
                arr, base = ((AR1, k) if k < 16 else
                             (AR2, k - 16) if k < 32 else (AR5, k - 32))
                return arr[:, base * 1024:(base + 1) * 1024]

            AR4f = AR4.ap().bitcast(F32)
            cm_mm = [P.cnt("s_t")] * 2
            tr_ps = [P.cnt("s_v")] * 2
            mk_b = [P.cnt("s_t")] * 2
            for stg in range(NTG):
                wait("tensor", "s_io", eot[-1])
                cbuf = (stg % 2) * 5120
                for ex in range(4):
                    for cp in range(10):
                        k = ex * 10 + cp
                        mk = sm_mask if k % 2 == 0 else sm_mask2
                        wait("vector", "s_t", mk_b[k % 2])
                        bld = vstep(lambda e, ex=ex, cp=cp, stg=stg, mk=mk:
                                    e.tensor_scalar(
                                        mk[:, :],
                                        c_iotac[:, cp * 128:(cp + 1) * 128],
                                        sm_posm[:, stg * 8 + ex:stg * 8 + ex + 1],
                                        sm_wcmb[:, stg * 8 + ex:stg * 8 + ex + 1],
                                        AL.is_equal, AL.mult))
                        ps = ps_a if k % 2 == 0 else ps_b
                        psv = ps.ap().bitcast(BF16)[:, 0:128]
                        wait("tensor", "s_v", max(bld, tr_ps[k % 2]))
                        mk_b[k % 2] = tstep(lambda e, mk=mk, psv=psv:
                                            e.transpose(psv, mk[:, :],
                                                        c_ident[:, :]))
                        wait("vector", "s_t", mk_b[k % 2])
                        wait("vector", "s_t", cm_mm[stg % 2])
                        tr_ps[k % 2] = vstep(
                            lambda e, k=k, psv=psv, cbuf=cbuf:
                            e.tensor_copy(AR4[:, cbuf + k * 128:
                                              cbuf + k * 128 + 128], psv))
                cmv = P.cnt("s_v")
                m6cp = getattr(emit_moe, "_m6cp", 0)
                for mh in range(2):
                    ps = ps_ctx if mh == 0 else ps_bc
                    wait("tensor", "s_v", cmv)
                    wait("tensor", "s_s")
                    wait("scalar", "s_io", m6cp)

                    def _mm(e, mh=mh, ps=ps, cbuf=cbuf):
                        last = None
                        for k in range(40):
                            last = e.matmul(ps.ap()[:, :],
                                            AR4[:, cbuf + k * 128:
                                                cbuf + k * 128 + 128],
                                            eo_tile(k)[:, mh * 512:mh * 512 + 512],
                                            start=(k == 0), stop=(k == 39))
                        return last
                    tt = tstep(_mm)
                    cm_mm[stg % 2] = tt
                    wait("scalar", "s_t", tt)
                    sstep(lambda e, mh=mh, ps=ps:
                          e.copy(AR4f[:, 5120 + mh * 512:5120 + mh * 512 + 512],
                                 ps.ap()[:, :]))
                stot = P.cnt("s_s")
                wait("sync", "s_s", stot)
                emit_moe._m6cp = cpair(
                    shr_m2[:, stg * D:(stg + 1) * D],
                    shr_m2[:, (NTG + stg) * D:(NTG + stg + 1) * D],
                    AR4f[:, 5120:6144])
            flag_write(FL_M2)
            flag_poll(FL_M2)
            m7_free = [P.cnt("s_v")] * 2
            m7_pp = [P.cnt("s_v")] * 2
            m7_hv = P.cnt("s_v")
            wait("sync", "s_t")
            for st in range(NT):
                ht = htile if st % 2 == 0 else htile2
                pp0 = (AR4f[:, 6144:7168] if st % 2 == 0 else AR4f[:, 0:1024])
                pp1 = (AR4f[:, 7168:8192] if st % 2 == 0 else AR4f[:, 1024:2048])
                wait("sync", "s_v", max(m7_hv, m7_free[st % 2]))
                t1 = dma("sync", ht, h_dram[st * 128:(st + 1) * 128, :])
                wait("sync", "s_v", m7_pp[st % 2])
                t2 = cpair(pp0, pp0,
                           shr_m2[:, st * D:(st + 1) * D],
                           shr_m2[:, (NT + st) * D:(NT + st + 1) * D])
                t3 = cpair(pp1, pp1,
                           shr_m2[:, (NTG + st) * D:(NTG + st + 1) * D],
                           shr_m2[:, (NTG + NT + st) * D:(NTG + NT + st + 1) * D])
                wait("vector", "s_io", t3)
                vstep(lambda e, pp0=pp0, ht=ht: e.tensor_add(ht, ht, pp0))
                wait("vector", "s_v")
                m7_pp[st % 2] = vstep(lambda e, pp1=pp1, ht=ht:
                                      e.tensor_add(ht, ht, pp1))
                vt = P.cnt("s_v")
                m7_free[st % 2] = vt
                wait("sync", "s_v", vt)
                dma("sync", y_out[st * 128:(st + 1) * 128, :], ht)
            wait("sync", "s_io")

        psl8 = [ps_s0.ap()[:, 0:512], ps_s0.ap()[:, 512:1024],
                ps_s1.ap()[:, 0:512], ps_s1.ap()[:, 512:1024],
                ps_ctx.ap()[:, :], ps_bc.ap()[:, :],
                ps_a.ap()[:, :], ps_b.ap()[:, :]]

        for l in range(_NLB):
            if _STAGE == "ln0":
                break
            wait("sync", "s_t")
            dma("sync", sm_bvrow[0:1, :], bvrow_in[0:1, l * 512:(l + 1) * 512])
            dma("sync", sm_borow[0:1, :], borow_in[0:1, l * D:(l + 1) * D])
            bias_tot = dma("sync", sm_b2row[0:1, :], b2row_in[0:1, l * D:(l + 1) * D])
            wait("tensor", "s_io", bias_tot)
            ln_pass(l, 1, AR1)
            vt = P.cnt("s_v")
            wait("sync", "s_v", vt)
            sxn = shr_xn[l].rearrange("p (mt s) -> p mt s", mt=NM)
            src1 = AR1.ap().rearrange("p (mt s) -> p mt s", mt=NM)
            cpair(sxn[:, :, 0:SH], sxn[:, :, SH:S], src1)
            flag_write(2 * l)
            phase_qkv(l)
            phase_attn(l)
            if _STAGE == "ctx0":
                wait("sync", "s_v")
                t = dma("sync", dbg_out, AR1.ap())
                wait("sync", "s_io", t)
                break
            phase_wo(l)
            if _STAGE == "mid0":
                break
            phase_ffn(l)

        if _STAGE in ("ln0", "mid0", "l0", "ctx0", "attn4"):
            wait("sync", "s_v")
            wait("sync", "s_io")
            ytot = dma("sync", y_out, h_dram[:, :])
            wait("sync", "s_io", ytot)
        else:
            emit_moe()

        @block.sync
        def _(eng):
            P.play("sync", eng)

        @block.gpsimd
        def _(eng):
            P.play("gpsimd", eng)

        @block.vector
        def _(eng):
            P.play("vector", eng)

        @block.scalar
        def _(eng):
            P.play("scalar", eng)

        @block.tensor
        def _(eng):
            P.play("tensor", eng)

    return nc


def _host_consts():
    import ml_dtypes
    bf = ml_dtypes.bfloat16
    inv = (1.0 / 10000.0 ** (np.arange(0, DH, 2, dtype=np.float64) / DH))
    pos = np.arange(S, dtype=np.float64)
    ang = pos[None, :] * inv[:, None]
    sin_t = np.sin(ang).astype(np.float32)
    cos_t = np.cos(ang).astype(np.float32)
    c = {}
    c["trig"] = np.concatenate([sin_t, cos_t, cos_t, sin_t],
                               axis=0).astype(bf)
    k_i = np.arange(128)
    c["tri"] = (k_i[:, None] <= k_i[None, :]).astype(bf)
    c["ltri"] = (k_i[:, None] <= k_i[None, :]).astype(bf)
    c["identb"] = np.eye(128, dtype=bf)
    c["identf"] = np.eye(128, dtype=np.float32)
    c["onesb"] = np.ones((128, 128), dtype=bf)
    c["onesf"] = np.ones((1, 512), dtype=np.float32)
    c["iotac"] = np.tile(np.arange(1, CAP + 1, dtype=np.float32)[None, :],
                         (128, 1))
    c["epsiota"] = np.tile((np.arange(E, dtype=np.float32) * 1e-10)[None, :],
                           (128, 1))
    return c


def kernel(L0_attn_x, L0_attn_Wq, L0_attn_bq, L0_attn_Wk, L0_attn_bk,
           L0_attn_Wv, L0_attn_bv, L0_attn_Wo, L0_attn_bo, L0_attn_u_bias,
           L0_attn_W1, L0_attn_b1, L0_attn_W2, L0_attn_b2, L0_attn_ln1_w,
           L0_attn_ln1_b, L0_attn_ln2_w, L0_attn_ln2_b, L0_ffn_gate_w,
           L0_ffn_w1, L0_ffn_w2, L0_ffn_w3):
    import ml_dtypes
    bf = ml_dtypes.bfloat16
    f32 = np.float32
    ascon = np.ascontiguousarray

    if "nc" not in _CACHE:
        _CACHE["nc"] = build_nc()
        _CACHE["consts"] = _host_consts()
    nc = _CACHE["nc"]
    consts = _CACHE["consts"]
    _CACHE["epoch"] = _CACHE.get("epoch", np.random.randint(1, 2 ** 30)) + 1
    epoch = np.full((1, 16), (_CACHE["epoch"] * 2654435761 + 12345) & 0x7FFFFFFF,
                    dtype=np.int32)

    x = np.asarray(L0_attn_x, f32)
    Wq = np.asarray(L0_attn_Wq, f32)
    Wk = np.asarray(L0_attn_Wk, f32)
    Wv = np.asarray(L0_attn_Wv, f32)
    Wo = np.asarray(L0_attn_Wo, f32)
    bq = np.asarray(L0_attn_bq, f32)
    bk = np.asarray(L0_attn_bk, f32)
    bv = np.asarray(L0_attn_bv, f32)
    bo = np.asarray(L0_attn_bo, f32)
    ub = np.asarray(L0_attn_u_bias, f32)
    W1 = np.asarray(L0_attn_W1, f32)
    b1 = np.asarray(L0_attn_b1, f32)
    W2 = np.asarray(L0_attn_W2, f32)
    b2 = np.asarray(L0_attn_b2, f32)
    l1w = np.asarray(L0_attn_ln1_w, f32)
    l1b = np.asarray(L0_attn_ln1_b, f32)
    l2w = np.asarray(L0_attn_ln2_w, f32)
    l2b = np.asarray(L0_attn_ln2_b, f32)
    gw = np.asarray(L0_ffn_gate_w, f32)
    mw1 = np.asarray(L0_ffn_w1, f32)
    mw2 = np.asarray(L0_ffn_w2, f32)
    mw3 = np.asarray(L0_ffn_w3, f32)

    prep = {}
    for hh in range(2):
        hs = slice(512 * hh, 512 * hh + 512)
        p = {}
        p["wqT"] = ascon(Wq[:, hs, :].transpose(0, 2, 1))
        p["wkT"] = ascon(Wk[:, hs, :].transpose(0, 2, 1))
        p["wvT"] = ascon(Wv[:, hs, :].transpose(0, 2, 1))
        p["woTm"] = ascon(Wo[:, :, hs].transpose(0, 2, 1))
        ps_ = slice(512 * (1 - hh), 512 * (1 - hh) + 512)
        p["woTp"] = ascon(Wo[:, :, ps_].transpose(0, 2, 1))
        bqkv = np.zeros((NL, 1536), f32)
        for l in range(NL):
            for wi, bb in enumerate((bq, bk, bv)):
                bqkv[l, wi * 512:(wi + 1) * 512] = bb[l, 512 * hh:512 * (hh + 1)]
        p["bqkv"] = bqkv
        uT = np.zeros((64, NL * 8), f32)
        for l in range(NL):
            for h in range(8):
                uT[:, l * 8 + h] = ub[l, 8 * hh + h, :]
        p["uT"] = uT
        eperm = list(range(4 * hh, 4 * hh + 4)) + list(range(4 * (1 - hh),
                                                             4 * (1 - hh) + 4))
        gperm = gw[eperm, :]
        gT = ascon(gperm.T)
        gtile = np.zeros((128, NM * E), f32)
        for mt in range(NM):
            gtile[:, mt * E:(mt + 1) * E] = gT[mt * 128:(mt + 1) * 128, :]
        p["gatewT"] = gtile.astype(bf)
        p["mw1"] = ascon(mw1[eperm[:4]])
        p["mw3"] = ascon(mw3[eperm[:4]])
        p["mw2"] = ascon(mw2[eperm[:4]])
        prep[hh] = p

    w1T = ascon(W1.transpose(0, 2, 1))
    w2T = ascon(W2.transpose(0, 2, 1))
    b1T = np.zeros((128, NL * 32), f32)
    for l in range(NL):
        for hc in range(32):
            b1T[:, l * 32 + hc] = b1[l, hc * 128:(hc + 1) * 128]
    lncols = np.zeros((128, NL * 4 * NM), f32)
    for l in range(NL):
        for wi, vv in enumerate((l1w, l1b, l2w, l2b)):
            for mt in range(NM):
                lncols[:, (l * 4 + wi) * NM + mt] = vv[l, mt * 128:(mt + 1) * 128]
    borow = bo.reshape(1, NL * D).astype(f32)
    b2row = b2.reshape(1, NL * D).astype(f32)
    bvrow = np.zeros((1, NL * 512), f32)

    in_maps = []
    for c in range(8):
        b_, hh = divmod(c, 2)
        p = prep[hh]
        m = {
            "x_my": ascon(x[b_, SH * hh:SH * (hh + 1), :]),
            "epoch": epoch,
            "wqT": p["wqT"], "wkT": p["wkT"], "wvT": p["wvT"],
            "woTm": p["woTm"], "woTp": p["woTp"],
            "w1T": w1T, "w2T": w2T,
            "bqkv": p["bqkv"], "uT": p["uT"], "b1T": b1T, "lncols": lncols,
            "borow": borow, "b2row": b2row,
            "bvrow": ascon(bv[:, 512 * hh:512 * (hh + 1)].reshape(1, NL * 512)),
            "cvals": np.tile(np.array([1e-5, 1e-6, -0.5, 0.125, -12.0, -1e9,
                                       CAP + 0.5, 0.0], np.float32)[None, :],
                             (128, 1)),
            "trig": consts["trig"],
            "tri": consts["tri"], "ltri": consts["ltri"],
            "identb": consts["identb"], "identf": consts["identf"],
            "onesb": consts["onesb"], "onesf": consts["onesf"],
            "iotac": consts["iotac"],
            "epsiota": consts["epsiota"],
            "gatewT": p["gatewT"],
            "mw1": p["mw1"], "mw3": p["mw3"], "mw2": p["mw2"],
        }
        in_maps.append(m)

    res = run_bass_kernel_spmd(nc, in_maps, core_ids=list(range(8)))
    if _STAGE == "ctx0":
        _CACHE["dbg"] = [np.asarray(res.results[c]["dbg"]) for c in range(8)]
    out = np.empty((B, S, D), f32)
    for c in range(8):
        b_, hh = divmod(c, 2)
        out[b_, SH * hh:SH * (hh + 1), :] = res.results[c]["y"]
    return out


# version 06 (final)
# speedup vs baseline: 1.1146x; 1.1146x over previous
import numpy as np
import concourse.bass as bass
import concourse.mybir as mybir
from concourse.bass_utils import run_bass_kernel_spmd

F32 = mybir.dt.float32
BF16 = mybir.dt.bfloat16
I32 = mybir.dt.int32
AF = mybir.ActivationFunctionType
AL = mybir.AluOpType
AX = mybir.AxisListType
ET = mybir.EngineType

B, S, D, H, DH, DI, NL = 4, 4096, 1024, 16, 64, 4096, 4
E, FF, CAP = 8, 3584, 1280
SH = S // 2
NT = SH // 128
NTG = S // 128
NM = D // 128
NQB = S // 512
NHC = FF // 128
EXP_SHIFT = 12.0

_STAGE = "full"
_NLB = {"ln0": 1, "mid0": 1, "l0": 1, "ctx0": 1, "attn4": NL, "full": NL}[_STAGE]

_CACHE = {}


class Prog:
    ENGINES = ("sync", "gpsimd", "vector", "scalar", "tensor")

    def __init__(self):
        self.steps = {e: [] for e in self.ENGINES}
        self.ctr = {}
        self.last_wait = {}

    def inc(self, s, n):
        self.ctr[s] = self.ctr.get(s, 0) + n
        return self.ctr[s]

    def cnt(self, s):
        return self.ctr.get(s, 0)

    def on(self, engine, fn):
        self.steps[engine].append(fn)

    def play(self, engine, eng):
        for fn in self.steps[engine]:
            fn(eng)


def build_nc():
    nc = bass.Bass("TRN2", target_bir_lowering=False, debug=False)
    P = Prog()

    def din(name, shape, dt=F32):
        return nc.dram_tensor(name, list(shape), dt, kind="ExternalInput").ap()

    x_in = din("x_my", (SH, D))
    ep_in = din("epoch", (1, 16), I32)
    wqT_in = din("wqT", (NL, D, 512))
    wkT_in = din("wkT", (NL, D, 512))
    wvT_in = din("wvT", (NL, D, 512))
    woTm_in = din("woTm", (NL, 512, D))
    woTp_in = din("woTp", (NL, 512, D))
    w1T_in = din("w1T", (NL, D, DI))
    w2T_in = din("w2T", (NL, DI, D))
    bqkv_in = din("bqkv", (NL, 1536))
    u_in = din("uT", (64, NL * 8))
    b1_in = din("b1T", (128, NL * 32))
    lncols_in = din("lncols", (128, NL * 4 * NM))
    borow_in = din("borow", (1, NL * D))
    b2row_in = din("b2row", (1, NL * D))
    bvrow_in = din("bvrow", (1, NL * 512))
    trig_in = din("trig", (128, S), BF16)
    tri_in = din("tri", (128, 128), BF16)
    ltri_in = din("ltri", (128, 128), BF16)
    identb_in = din("identb", (128, 128), BF16)
    identf_in = din("identf", (128, 128), F32)
    onesb_in = din("onesb", (128, 128), BF16)
    onesf_in = din("onesf", (1, 512), F32)
    iotac_in = din("iotac", (128, CAP))
    epsi_in = din("epsiota", (128, E))
    gatewT_in = din("gatewT", (128, NM * E), BF16)
    cvals_in = din("cvals", (128, 8))
    mw1_in = din("mw1", (4, D, FF))
    mw3_in = din("mw3", (4, D, FF))
    mw2_in = din("mw2", (4, FF, D))

    y_out = nc.dram_tensor("y", [SH, D], F32, kind="ExternalOutput").ap()
    dbg_out = (nc.dram_tensor("dbg", [128, 16384], BF16, kind="ExternalOutput").ap()
               if _STAGE == "ctx0" else None)

    h_dram = nc.dram_tensor("h_dram", [SH, D], F32).ap()
    eo_dram = nc.dram_tensor("eo_dram", [4 * CAP, D], BF16).ap()

    shr_xn = [nc.dram_tensor(f"shr_xn{l}", [128, NM * S], BF16,
                             addr_space="Shared").ap() for l in range(_NLB)]
    shr_ctx = [nc.dram_tensor(f"shr_ctx{l}", [128, 2 * 4 * S], BF16,
                              addr_space="Shared").ap() for l in range(_NLB)]
    shr_m1 = nc.dram_tensor("shr_m1", [128, NTG * D], BF16, addr_space="Shared").ap()
    shr_m2 = nc.dram_tensor("shr_m2", [128, 2 * NTG * D], F32, addr_space="Shared").ap()
    shr_fl = [nc.dram_tensor(f"shr_fl{k}", [1, 32], I32, addr_space="Shared").ap()
              for k in range(10)]

    FL_M1, FL_M2 = 8, 9

    from contextlib import ExitStack
    with ExitStack() as _stk:
        AR1 = _stk.enter_context(nc.sbuf_tensor("AR1", [128, 16384], BF16))
        AR2 = _stk.enter_context(nc.sbuf_tensor("AR2", [128, 16384], BF16))
        AR3 = _stk.enter_context(nc.sbuf_tensor("AR3", [128, 16640], BF16))
        AR4 = _stk.enter_context(nc.sbuf_tensor("AR4", [128, 16384], BF16))
        AR5 = _stk.enter_context(nc.sbuf_tensor("AR5", [128, 8192], BF16))
        AR7 = _stk.enter_context(nc.sbuf_tensor("AR7", [128, 4096], BF16))
        ARF = _stk.enter_context(nc.sbuf_tensor("ARF", [128, 1024], F32))
        PTB = _stk.enter_context(nc.sbuf_tensor("PTB", [128, 1024], BF16))
        c_trig = _stk.enter_context(nc.sbuf_tensor("c_trig", [128, S], BF16))
        c_tri = _stk.enter_context(nc.sbuf_tensor("c_tri", [128, 128], BF16))
        c_ltri = _stk.enter_context(nc.sbuf_tensor("c_ltri", [128, 128], BF16))
        c_ident = _stk.enter_context(nc.sbuf_tensor("c_ident", [128, 128], BF16))
        c_identf = _stk.enter_context(nc.sbuf_tensor("c_identf", [128, 128], F32))
        c_onesb = _stk.enter_context(nc.sbuf_tensor("c_onesb", [128, 128], BF16))
        c_onesf = _stk.enter_context(nc.sbuf_tensor("c_onesf", [1, 512], F32))
        sm_bqkvrow = _stk.enter_context(nc.sbuf_tensor("sm_bqkvrow", [1, 512], F32))
        c_u = _stk.enter_context(nc.sbuf_tensor("c_u", [64, NL * 8], F32))
        c_b1 = _stk.enter_context(nc.sbuf_tensor("c_b1", [128, NL * 32], F32))
        c_ln = _stk.enter_context(nc.sbuf_tensor("c_ln", [128, NL * 4 * NM], F32))
        sm_borow = _stk.enter_context(nc.sbuf_tensor("sm_borow", [1, D], F32))
        sm_b2row = _stk.enter_context(nc.sbuf_tensor("sm_b2row", [1, D], F32))
        sm_bvrow = _stk.enter_context(nc.sbuf_tensor("sm_bvrow", [1, 512], F32))
        c_eps = _stk.enter_context(nc.sbuf_tensor("c_eps", [128, E], F32))
        c_gatewT = _stk.enter_context(nc.sbuf_tensor("c_gatewT", [128, NM * E], BF16))
        c_iotac = _stk.enter_context(nc.sbuf_tensor("c_iotac", [128, CAP], F32))
        c_cvals = _stk.enter_context(nc.sbuf_tensor("c_cvals", [128, 8], F32))
        ep_sb = _stk.enter_context(nc.sbuf_tensor("ep_sb", [1, 16], I32))
        pf_sb = _stk.enter_context(nc.sbuf_tensor("pf_sb", [1, 32], I32))
        sm_stats = _stk.enter_context(nc.sbuf_tensor("sm_stats", [128, 12], F32))
        sm_mr = _stk.enter_context(nc.sbuf_tensor("sm_mr", [128, 8], F32))
        sm_ctx = _stk.enter_context(nc.sbuf_tensor("sm_ctx", [64, 512], BF16))
        T_A = _stk.enter_context(nc.sbuf_tensor("T_A", [128, 2048], BF16))
        T_B = _stk.enter_context(nc.sbuf_tensor("T_B", [128, 2048], BF16))
        T_C = _stk.enter_context(nc.sbuf_tensor("T_C", [64, 2048], BF16))
        sm_probs = _stk.enter_context(nc.sbuf_tensor("sm_probs", [128, NTG * E], F32))
        sm_posm = _stk.enter_context(nc.sbuf_tensor("sm_posm", [128, NTG * E], F32))
        sm_wcmb = _stk.enter_context(nc.sbuf_tensor("sm_wcmb", [128, NTG * E], F32))
        sm_cin = _stk.enter_context(nc.sbuf_tensor("sm_cin", [128, NTG * E], F32))
        sm_g = _stk.enter_context(nc.sbuf_tensor("sm_g", [128, NTG * E], BF16))
        sm_gf = _stk.enter_context(nc.sbuf_tensor("sm_gf", [128, NTG * E], F32))
        sm_tot = _stk.enter_context(nc.sbuf_tensor("sm_tot", [128, 144], F32))
        ps_s0 = _stk.enter_context(nc.psum_tensor("ps_s0", [128, 1024], F32))
        ps_s1 = _stk.enter_context(nc.psum_tensor("ps_s1", [128, 1024], F32))
        ps_ctx = _stk.enter_context(nc.psum_tensor("ps_ctx", [128, 512], F32))
        ps_bc = _stk.enter_context(nc.psum_tensor("ps_bc", [128, 512], F32))
        ps_a = _stk.enter_context(nc.psum_tensor("ps_a", [128, 512], F32))
        ps_b = _stk.enter_context(nc.psum_tensor("ps_b", [128, 512], F32))
        s_io = _stk.enter_context(nc.semaphore("s_io"))
        s_w = _stk.enter_context(nc.semaphore("s_w"))
        s_p = _stk.enter_context(nc.semaphore("s_p"))
        s_t = _stk.enter_context(nc.semaphore("s_t"))
        s_s = _stk.enter_context(nc.semaphore("s_s"))
        s_v = _stk.enter_context(nc.semaphore("s_v"))
        block = _stk.enter_context(nc.Block())
        SEM = {"s_io": s_io, "s_w": s_w, "s_t": s_t, "s_s": s_s, "s_v": s_v}
        xtile = AR3.ap().bitcast(F32)[:, 0:1024]
        sm_rz = sm_bqkvrow.ap()
        PT2 = c_iotac.ap().bitcast(BF16)[:, 0:1024]
        sm_erow = sm_bqkvrow.ap()[0:1, 0:256]
        sm_mask = T_A.ap()[:, 0:128]
        sm_mask2 = T_A.ap()[:, 128:256]
        silu_sc = PTB.ap().bitcast(F32)[:, 0:512]
        sm_xt = PTB.ap()
        htile = ARF[:, 0:1024]
        htile2 = AR7.ap().bitcast(F32)[:, 0:1024]

        def wait(engine, sname, tot=None):
            if tot is None:
                tot = P.cnt(sname)
            key = (engine, sname)
            if P.last_wait.get(key, -1) >= tot:
                return
            P.last_wait[key] = tot
            sem = SEM[sname]
            P.on(engine, lambda e, sem=sem, tot=tot: e.wait_ge(sem, tot))

        def dma(eng_name, dst, src, slow=False):
            sname = "s_w" if eng_name == "gpsimd" else "s_io"
            sem = SEM[sname]
            kw = {"allow_slow_non_contiguous": True} if slow else {}

            def _go(e, d=dst, s=src, sem=sem, kw=kw):
                e.dma_start(out=d, in_=s, **kw).then_inc(sem, 16)
            P.on(eng_name, _go)
            return P.inc(sname, 16)

        def vstep(fn, n=1):
            P.on("vector", lambda e, fn=fn: fn(e).then_inc(s_v, 1))
            return P.inc("s_v", n)

        def sstep(fn, n=1):
            P.on("scalar", lambda e, fn=fn: fn(e).then_inc(s_s, 1))
            return P.inc("s_s", n)

        def tstep(fn, n=1):
            P.on("tensor", lambda e, fn=fn: fn(e).then_inc(s_t, 1))
            return P.inc("s_t", n)

        pstate = {}

        def init_parity(e):
            pid = nc.partition_id([ET.SP])
            pstate["par"] = e.compute_val(pid % 2)
            pstate["rpar"] = e.alloc_register("rpar")
            e.reg_mov(pstate["rpar"], pstate["par"])
            pstate["r_flag"] = e.alloc_register("r_flag")
            pstate["r_ep"] = e.alloc_register("r_ep")
            pstate["r_exp"] = e.alloc_register("r_exp")
            pstate["r_cond"] = e.alloc_register("r_cond")
            pstate["r_flag2"] = e.alloc_register("r_flag2")
            pstate["r_cond2"] = e.alloc_register("r_cond2")
            e.reg_mov(pstate["r_exp"], 0)

        def cpair(dst_even, dst_odd, src_even, src_odd=None):
            if src_odd is None:
                src_odd = src_even

            def _go(e, de=dst_even, do=dst_odd, se=src_even, so=src_odd):
                with e.If_eq(pstate["rpar"], 0):
                    e.dma_start(out=de, in_=se).then_inc(s_io, 16)
                with e.Else():
                    e.dma_start(out=do, in_=so).then_inc(s_io, 16)
            P.on("sync", _go)
            return P.inc("s_io", 16)

        def flag_write(event):
            wait("sync", "s_io")
            cpair(shr_fl[event][0:1, 0:16], shr_fl[event][0:1, 16:32], ep_sb[:, :])

        def flag_poll(event):
            def _go(e, ev=event):
                rf, rep, rex, rc = (pstate["r_flag"], pstate["r_ep"],
                                    pstate["r_exp"], pstate["r_cond"])
                rf2 = pstate["r_flag2"]
                rc2 = pstate["r_cond2"]
                e.reg_mov(rc, 1)
                with e.While(rc):
                    e.dma_start(out=pf_sb[0:1, 0:32],
                                in_=shr_fl[ev][0:1, 0:32]).then_inc(s_p, 16)
                    e.reg_add(rex, rex, 16)
                    e.wait_ge(s_p, rex)
                    e.load(rf, pf_sb[0:1, 0:1])
                    e.load(rf2, pf_sb[0:1, 16:17])
                    e.reg_alu(rc, rf, rep, AL.not_equal)
                    e.reg_alu(rc2, rf2, rep, AL.not_equal)
                    e.reg_alu(rc, rc, rc2, AL.add)
            P.on("sync", _go)

        for dst, src in [(c_trig[:, :], trig_in), (c_tri[:, :], tri_in),
                         (c_ltri[:, :], ltri_in), (c_ident[:, :], identb_in),
                         (c_identf[:, :], identf_in), (c_onesb[:, :], onesb_in),
                         (c_onesf[:, :], onesf_in),
                         (c_u[:, :], u_in), (c_b1[:, :], b1_in),
                         (c_ln[:, :], lncols_in),
                         (c_eps[:, :], epsi_in), (c_gatewT[:, :], gatewT_in),
                         (c_iotac[:, :], iotac_in),
                         (c_cvals[:, :], cvals_in)]:
            dma("gpsimd", dst, src)
        consts_w = P.cnt("s_w")

        CVALS = [1e-5, 1e-6, -0.5, 0.125, -float(EXP_SHIFT), -1e9, CAP + 0.5, 0.0]
        for ci, cv in enumerate(CVALS):
            nc.const_aps.register(F32, float(cv), c_cvals[:, ci:ci + 1])

        P.on("sync", init_parity)
        ep_tot = dma("sync", ep_sb[:, :], ep_in)
        P.on("sync", lambda e: (e.wait_ge(s_io, ep_tot),
                                e.load(pstate["r_ep"], ep_sb[0:1, 0:1])))
        P.last_wait[("sync", "s_io")] = ep_tot

        wait("vector", "s_w", consts_w)
        wait("scalar", "s_w", consts_w)
        wait("tensor", "s_w", consts_w)

        def ln_core(src, dst, eps):
            P.on("vector", lambda e: e.bn_stats(sm_stats[:, 0:6], src[:, 0:512]))
            vstep(lambda e: e.bn_stats(sm_stats[:, 6:12], src[:, 512:1024]))
            wait("vector", "s_v")
            vt = vstep(lambda e: e.bn_aggr(sm_mr[:, 0:2], sm_stats[:, 0:12]))
            wait("scalar", "s_v", vt)
            st1 = sstep(lambda e: e.activation(sm_mr[:, 2:3], sm_mr[:, 1:2],
                                               AF.Ln, bias=float(eps)))
            wait("scalar", "s_s", st1)
            st2 = sstep(lambda e: e.activation(sm_mr[:, 3:4], sm_mr[:, 2:3],
                                               AF.Exp, scale=-0.5))
            wait("vector", "s_s", st2)
            return vstep(lambda e, dst=dst, src=src:
                         e.tensor_scalar(dst, src, sm_mr[:, 0:1], sm_mr[:, 3:4],
                                         AL.subtract, AL.mult))

        ln0_free = [0, 0]
        ln0_hv = P.cnt("s_v")
        for st in range(NT):
            ht = htile if st % 2 == 0 else htile2
            wait("sync", "s_v", ln0_hv)
            wait("sync", "s_io", ln0_free[st % 2])
            tot = dma("sync", ht, x_in[st * 128:(st + 1) * 128, :])
            wait("vector", "s_io", tot)
            vt = ln_core(ht, ht, 1e-5)
            wait("sync", "s_v", vt)
            ln0_free[st % 2] = dma("sync", h_dram[st * 128:(st + 1) * 128, :], ht)

        def ln_pass(l, which, dstT):
            wcol0 = (l * 4 + (0 if which == 1 else 2)) * NM
            bcol0 = wcol0 + NM
            wait("sync", "s_io")
            lnht_free = [0, 0]
            lnps = {}
            last_tr = P.cnt("s_t")
            ln_hv = P.cnt("s_v")
            lnps0 = P.cnt("s_v")
            for st in range(NT):
                ht = htile if st % 2 == 0 else htile2
                wait("sync", "s_v", max(ln_hv, lnht_free[st % 2]))
                tot = dma("sync", ht, h_dram[st * 128:(st + 1) * 128, :])
                wait("vector", "s_io", tot)
                wait("vector", "s_t", last_tr)
                vt = ln_core(ht, AR5[:, 0:1024], 1e-6)
                lnht_free[st % 2] = vt
                for mt in range(NM):
                    pix = mt % 2
                    ps = ps_a if pix == 0 else ps_b
                    psv = ps.ap().bitcast(BF16)[:, 0:128]
                    wait("tensor", "s_v", max(vt, lnps.get(pix, lnps0)))
                    tt = tstep(lambda e, psv=psv, mt=mt:
                               e.transpose(psv, AR5[:, mt * 128:(mt + 1) * 128],
                                           c_ident[:, :]))
                    wait("vector", "s_t", tt)
                    wc = c_ln[:, wcol0 + mt:wcol0 + mt + 1]
                    bc = c_ln[:, bcol0 + mt:bcol0 + mt + 1]
                    dc = mt * SH + st * 128
                    lnps[pix] = vstep(
                        lambda e, psv=psv, dc=dc, wc=wc, bc=bc, dstT=dstT:
                        e.tensor_scalar(dstT[:, dc:dc + 128], psv, wc, bc,
                                        AL.mult, AL.add))
                    last_tr = P.cnt("s_t")

        def phase_qkv(l):
            wait("vector", "s_t")
            vstep(lambda e: e.memset(AR3[:, :], 1.0))
            projs = [(wqT_in, AR2, 0, "qk"), (wkT_in, AR4, 1, "qk"),
                     (wvT_in, None, 2, "v")]
            for pi, (w_in, dstT, bofs, kind) in enumerate(projs):
                wait("gpsimd", "s_t")
                wait("gpsimd", "s_v")
                wtot = dma("gpsimd",
                           AR7.ap().rearrange("p (mt d) -> p mt d", mt=NM),
                           w_in[l].rearrange("(mt p) d -> p mt d", p=128))
                wait("tensor", "s_w", wtot)
                if kind == "qk":
                    wait("sync", "s_t")
                    brtot = dma("sync", sm_bqkvrow[0:1, :],
                                bqkv_in[l:l + 1, bofs * 512:(bofs + 1) * 512])
                    wait("tensor", "s_io", brtot)
                if pi == 0:
                    c0 = P.cnt("s_t")
                    qkv_state = {"chunk": [c0, c0], "pse": {}, "psv": {},
                                 "pse0": P.cnt("s_s"), "psv0": P.cnt("s_v")}
                st_ = qkv_state
                for scg in range(8):
                    if pi == 0 and scg == 0:
                        wait("sync", "s_io")
                        flag_poll(2 * l)
                    buf = scg % 2
                    bufc = buf * 4096
                    wait("sync", "s_t", st_["chunk"][buf])
                    sxn = shr_xn[l].rearrange("p (mt s) -> p mt s", mt=NM)
                    ctot = dma("sync",
                               AR5.ap().rearrange("p (b mt s) -> p b mt s",
                                                  b=2, mt=NM)[:, buf],
                               sxn[:, :, scg * 512:(scg + 1) * 512])
                    wait("tensor", "s_io", ctot)
                    if kind == "qk":
                        for dch in range(4):
                            pix = (scg * 4 + dch) % 2
                            ps = ps_s0 if pix == 0 else ps_s1
                            psr = ps.ap()[:, 0:512]
                            wait("tensor", "s_s", st_["pse"].get(pix, st_["pse0"]))

                            brow = dch * 128

                            def _mm(e, psr=psr, dch=dch, bufc=bufc, brow=brow):
                                for mt in range(NM):
                                    e.matmul(
                                        psr,
                                        AR7[:, mt * 512 + dch * 128:
                                            mt * 512 + dch * 128 + 128],
                                        AR5[:, bufc + mt * 512:bufc + mt * 512 + 512],
                                        start=(mt == 0), stop=False)
                                return e.matmul(psr,
                                                sm_bqkvrow[0:1, brow:brow + 128],
                                                c_onesf[0:1, 0:512],
                                                start=False, stop=True)
                            tt = tstep(_mm)
                            wait("scalar", "s_t", tt)
                            dc = dch * 4096 + scg * 512
                            st_["pse"][pix] = sstep(
                                lambda e, psr=psr, dc=dc, dstT=dstT:
                                e.copy(dstT[:, dc:dc + 512], psr))
                            st_["chunk"][buf] = P.cnt("s_t")
                    else:
                        for sv in range(4):
                            stg = scg * 4 + sv
                            pix = stg % 2
                            ps = ps_s0 if pix == 0 else ps_s1
                            psr = ps.ap()[:, 0:512]
                            wait("tensor", "s_v", st_["psv"].get(pix, st_["psv0"]))

                            def _mm(e, psr=psr, sv=sv, bufc=bufc):
                                for mt in range(NM):
                                    e.matmul(psr,
                                             AR5[:, bufc + mt * 512 + sv * 128:
                                                 bufc + mt * 512 + sv * 128 + 128],
                                             AR7[:, mt * 512:mt * 512 + 512],
                                             start=(mt == 0), stop=False)
                                return e.matmul(psr, c_onesf[0:1, 0:128],
                                                sm_bvrow[0:1, :],
                                                start=False, stop=True)
                            tt = tstep(_mm)
                            wait("vector", "s_t", tt)
                            dstv = AR3.ap().rearrange("p (h k c) -> p h k c",
                                                      h=8, k=32)[:, :, stg, 0:64]
                            srcv = psr.rearrange("p (h c) -> p h c", h=8)
                            st_["psv"][pix] = vstep(
                                lambda e, dstv=dstv, srcv=srcv:
                                e.tensor_copy(dstv, srcv))
                            st_["chunk"][buf] = P.cnt("s_t")

        def phase_attn(l):
            ctx_war = P.cnt("s_s")
            wait("vector", "s_t")
            vstep(lambda e: e.tensor_copy(AR5[64:128, 4096:8192], c_trig[64:128, :]))
            for h in range(8):
                hrow = 64 * (h % 2)
                hcol = (h // 2) * 4096
                wait("vector", "s_t")
                vstep(lambda e, hrow=hrow, hcol=hcol:
                      e.tensor_copy(AR5[0:64, 0:4096],
                                    AR2[hrow:hrow + 64, hcol:hcol + 4096]))
                vstep(lambda e, hrow=hrow, hcol=hcol:
                      e.tensor_copy(AR5[0:64, 4096:8192],
                                    AR4[hrow:hrow + 64, hcol:hcol + 4096]))
                ucol = l * 8 + h
                for hp in range(2):
                    cc = hp * 2048
                    vstep(lambda e, cc=cc, ucol=ucol:
                          e.tensor_scalar(T_A[0:32, :], AR5[0:32, cc:cc + 2048],
                                          c_u[0:32, ucol:ucol + 1], None, AL.add))
                    vstep(lambda e, cc=cc, ucol=ucol:
                          e.tensor_scalar(T_A[32:64, :], AR5[32:64, cc:cc + 2048],
                                          c_u[32:64, ucol:ucol + 1], None, AL.add))
                    wait("vector", "s_v")
                    vstep(lambda e: e.tensor_copy(T_B[32:64, :], T_A[0:32, :]))
                    vstep(lambda e: e.tensor_copy(T_B[0:32, :], T_A[32:64, :]))
                    wait("vector", "s_v")
                    vstep(lambda e, cc=cc: e.tensor_mul(T_A[64:96, :], T_A[0:32, :],
                                                        c_trig[0:32, cc:cc + 2048]))
                    vstep(lambda e, cc=cc: e.tensor_mul(T_B[64:96, :], T_A[32:64, :],
                                                        c_trig[32:64, cc:cc + 2048]))
                    wait("vector", "s_v")
                    vstep(lambda e: e.tensor_add(T_C[0:32, :], T_A[64:96, :],
                                                 T_B[64:96, :]))
                    wait("vector", "s_v")
                    vstep(lambda e, cc=cc: e.tensor_mul(T_A[64:96, :], T_B[0:32, :],
                                                        c_trig[0:32, cc:cc + 2048]))
                    vstep(lambda e, cc=cc: e.tensor_mul(T_B[64:96, :], T_B[32:64, :],
                                                        c_trig[32:64, cc:cc + 2048]))
                    wait("vector", "s_v")
                    vstep(lambda e: e.tensor_sub(T_C[32:64, :], T_A[64:96, :],
                                                 T_B[64:96, :]))
                    wait("vector", "s_v")
                    vstep(lambda e, cc=cc:
                          e.tensor_copy(AR5[64:128, cc:cc + 2048], T_C[:, :]))
                asm_v = P.cnt("s_v")
                exp_done = [P.cnt("s_s")] * 2
                pv_done = [P.cnt("s_t")] * 2

                for qb in range(NQB):
                    nkt = 4 * qb + 4
                    npp = (nkt + 1) // 2
                    pend_pv = None

                    def mk_pv(kts, qb=qb, h=h, nkt=nkt, pt=None):
                        def _pv(e, kts=kts, qb=qb, h=h, nkt=nkt, pt=pt):
                            last = None
                            for j, kt in enumerate(kts):
                                dlt = max(0, (kt - 4 * qb) * 128)
                                vofs = (h * 32 + kt) * 65
                                last = e.matmul(
                                    ps_ctx[0:65, dlt:512],
                                    AR3[:, vofs:vofs + 65],
                                    pt[:, j * 512 + dlt:(j + 1) * 512],
                                    start=(kt == 0), stop=(kt == nkt - 1))
                            return last
                        return _pv

                    for pp in range(npp):
                        kts = [2 * pp + j for j in (0, 1) if 2 * pp + j < nkt]
                        ps = ps_s0 if pp % 2 == 0 else ps_s1
                        pt = PTB.ap() if pp % 2 == 0 else PT2
                        wait("tensor", "s_s", exp_done[pp % 2])
                        wait("tensor", "s_v", asm_v)

                        def _sc(e, kts=kts, qb=qb, ps=ps):
                            last = None
                            for j, kt in enumerate(kts):
                                dlt = max(0, (kt - 4 * qb) * 128)
                                last = e.matmul(
                                    ps.ap()[:, j * 512 + dlt:(j + 1) * 512],
                                    AR5[:, 4096 + kt * 128:4096 + kt * 128 + 128],
                                    AR5[:, qb * 512 + dlt:qb * 512 + 512],
                                    start=True, stop=True)
                            return last
                        tt = tstep(_sc)
                        if pend_pv is not None:
                            if pp == 1:
                                wait("tensor", "s_s", ctx_war)
                            tstep(pend_pv[0])
                            pv_done[pend_pv[1]] = P.cnt("s_t")
                        wait("scalar", "s_t", max(tt, pv_done[pp % 2]))

                        def _exp(e, kts=kts, qb=qb, ps=ps, pt=pt):
                            last = None
                            for j, kt in enumerate(kts):
                                dlt = max(0, (kt - 4 * qb) * 128)
                                last = e.activation(
                                    pt[:, j * 512 + dlt:(j + 1) * 512],
                                    ps.ap()[:, j * 512 + dlt:(j + 1) * 512],
                                    AF.Exp, bias=-EXP_SHIFT, scale=0.125)
                            return last
                        et = sstep(_exp)
                        exp_done[pp % 2] = et
                        need_v = any(kt >= 4 * qb for kt in kts)
                        if need_v:
                            wait("vector", "s_s", et)

                            def _msk(e, kts=kts, qb=qb, pt=pt):
                                last = None
                                for j, kt in enumerate(kts):
                                    if kt >= 4 * qb:
                                        dlt = (kt - 4 * qb) * 128
                                        last = e.tensor_mul(
                                            pt[:, j * 512 + dlt:j * 512 + dlt + 128],
                                            pt[:, j * 512 + dlt:j * 512 + dlt + 128],
                                            c_tri[:, :])
                                return last
                            mt_ = vstep(_msk)
                            wait("tensor", "s_v", mt_)
                        wait("tensor", "s_s", et)
                        pend_pv = (mk_pv(kts, pt=pt), pp % 2)
                    tstep(pend_pv[0])
                    pv_done[pend_pv[1]] = P.cnt("s_t")
                    wait("vector", "s_t")
                    rv = vstep(lambda e: e.reciprocal(sm_rz[0:1, :],
                                                      ps_ctx[64:65, :]))
                    wait("tensor", "s_v", rv)
                    bt_ = tstep(lambda e: e.matmul(ps_bc[0:64, :],
                                                   c_onesf[0:1, 0:64],
                                                   sm_rz[0:1, :],
                                                   start=True, stop=True))
                    wait("scalar", "s_t", bt_)
                    ct_ = sstep(lambda e: e.copy(sm_ctx[:, :], ps_ctx[0:64, :]))
                    ctx_war = ct_
                    wait("vector", "s_t", bt_)
                    wait("vector", "s_s", ct_)
                    dc = hcol + qb * 512
                    vstep(lambda e, hrow=hrow, dc=dc:
                          e.tensor_mul(AR1[hrow:hrow + 64, dc:dc + 512],
                                       sm_ctx[:, :], ps_bc[0:64, :]))
            wait("tensor", "s_s")
            wait("tensor", "s_v")

        def phase_wo(l):
            a1 = AR1.ap().rearrange("p (d s) -> p d s", d=4)
            sc = shr_ctx[l].rearrange("p (o d s) -> p o d s", o=2, d=4)
            wait("sync", "s_v")
            cpair(sc[:, 0], sc[:, 1], a1)
            flag_write(2 * l + 1)
            wait("sync", "s_t")
            wait("sync", "s_io")
            a4 = AR4.ap().rearrange("p (d s) -> p d s", d=8)[:, 0:4]
            cpair(a4, a4, sc[:, 0, :, 0:SH], sc[:, 1, :, SH:S])
            wait("gpsimd", "s_v")
            w_m = dma("gpsimd",
                      AR2.ap()[:, 0:4096].rearrange("p (d m) -> p d m", d=4),
                      woTm_in[l].rearrange("(d p) m -> p d m", p=128))
            w_p = dma("gpsimd",
                      AR2.ap()[:, 8192:12288].rearrange("p (d m) -> p d m", d=4),
                      woTp_in[l].rearrange("(d p) m -> p d m", p=128))
            flag_poll(2 * l + 1)
            wait("sync", "s_t")
            a5 = AR5.ap().rearrange("p (d s) -> p d s", d=4)
            ptot = cpair(a5, a5, sc[:, 1, :, 0:SH], sc[:, 0, :, SH:S])
            wait("tensor", "s_io", ptot)
            wait("tensor", "s_w", w_p)
            psl4 = [ps_a, ps_b, ps_ctx, ps_bc]
            wo_pse = {}
            wo_pse0 = P.cnt("s_v")
            ht_free = [0, 0]
            wo_hv = P.cnt("s_v")
            for st in range(NT):
                ht = htile if st % 2 == 0 else htile2
                wait("sync", "s_v", wo_hv)
                wait("sync", "s_io", ht_free[st % 2])
                htot = dma("sync", ht, h_dram[st * 128:(st + 1) * 128, :])
                addv = 0
                for mh in range(2):
                    bidx = (st * 2 + mh) % 4
                    ps = psl4[bidx]
                    wait("tensor", "s_v", wo_pse.get(bidx, wo_pse0))

                    def _mm(e, st=st, mh=mh, ps=ps, l=l):
                        for dt_ in range(4):
                            e.matmul(ps.ap()[:, :],
                                     AR4[:, dt_ * 2048 + st * 128:
                                         dt_ * 2048 + st * 128 + 128],
                                     AR2[:, dt_ * 1024 + mh * 512:
                                         dt_ * 1024 + mh * 512 + 512],
                                     start=(dt_ == 0), stop=False)
                        for dt_ in range(4):
                            e.matmul(ps.ap()[:, :],
                                     AR5[:, dt_ * 2048 + st * 128:
                                         dt_ * 2048 + st * 128 + 128],
                                     AR2[:, 8192 + dt_ * 1024 + mh * 512:
                                         8192 + dt_ * 1024 + mh * 512 + 512],
                                     start=False, stop=False)
                        return e.matmul(ps.ap()[:, :], c_onesf[0:1, 0:128],
                                        sm_borow[0:1, mh * 512:mh * 512 + 512],
                                        start=False, stop=True)
                    tt = tstep(_mm)
                    wait("vector", "s_t", tt)
                    wait("vector", "s_io", htot)
                    addv = vstep(lambda e, mh=mh, ps=ps, ht=ht:
                                 e.tensor_add(ht[:, mh * 512:(mh + 1) * 512],
                                              ht[:, mh * 512:(mh + 1) * 512],
                                              ps.ap()[:, :]))
                    wo_pse[bidx] = addv
                wait("sync", "s_v", addv)
                ht_free[st % 2] = dma("sync", h_dram[st * 128:(st + 1) * 128, :], ht)

        def phase_ffn(l):
            wait("vector", "s_t")
            wait("scalar", "s_t")
            wait("scalar", "s_io")
            ln_pass(l, 2, AR4)
            wait("gpsimd", "s_t")
            w1c = []
            w1v = w1T_in[l].rearrange("(mt p) (hc x) -> p hc mt x", p=128, x=128)
            for hc in range(32):
                arr = AR2 if hc < 16 else AR3
                base = (hc % 16) * 1024
                w1c.append(dma("gpsimd",
                               arr.ap()[:, base:base + 1024].rearrange(
                                   "p (mt x) -> p mt x", x=128),
                               w1v[:, hc]))
            ffn_pse = {}
            ffn_ffv = {}
            ffn_pse0 = P.cnt("s_s")
            fnT_v = P.cnt("s_v")
            wait("tensor", "s_v", fnT_v)
            for sc_ in range(4):
                for hc in range(32):
                    pix = hc % 2
                    ps = ps_s0 if pix == 0 else ps_s1
                    psr = ps.ap()[:, 0:512]
                    arr = AR2 if hc < 16 else AR3
                    base = (hc % 16) * 1024
                    wait("tensor", "s_w", w1c[hc])
                    wait("tensor", "s_s")
                    wait("tensor", "s_v")

                    def _mm(e, psr=psr, arr=arr, base=base, sc_=sc_):
                        last = None
                        for mt in range(NM):
                            last = e.matmul(
                                psr, arr[:, base + mt * 128:base + mt * 128 + 128],
                                AR4[:, mt * SH + sc_ * 512:mt * SH + sc_ * 512 + 512],
                                start=(mt == 0), stop=(mt == NM - 1))
                        return last
                    tt = tstep(_mm)
                    wait("scalar", "s_t", tt)
                    bcol = c_b1[:, l * 32 + hc:l * 32 + hc + 1]
                    ffn_pse[pix] = sstep(
                        lambda e, psr=psr, hc=hc, bcol=bcol:
                        e.activation(AR1[:, hc * 512:hc * 512 + 512], psr,
                                     AF.Gelu, bias=bcol))
                psl = [ps_s0.ap()[:, 0:512], ps_s0.ap()[:, 512:1024],
                       ps_s1.ap()[:, 0:512], ps_s1.ap()[:, 512:1024],
                       ps_ctx.ap()[:, :], ps_bc.ap()[:, :],
                       ps_a.ap()[:, :], ps_b.ap()[:, :]]
                w2v = w2T_in[l].rearrange("(hg p) m -> p hg m", p=128)
                wait("tensor", "s_s")
                wait("tensor", "s_v")
                w2buf = getattr(phase_ffn, "_w2buf", [0, 0])
                wait("gpsimd", "s_v", fnT_v)
                for hg in range(8):
                    buf = hg % 2
                    wait("gpsimd", "s_t", w2buf[buf])
                    wtot = dma("gpsimd",
                               AR5.ap().rearrange("p (b c m) -> p b c m",
                                                  b=2, c=4)[:, buf],
                               w2v[:, hg * 4:(hg + 1) * 4])
                    wait("tensor", "s_w", wtot)

                    def _mm(e, hg=hg, buf=buf, sc_=sc_):
                        last = None
                        for hx in range(4):
                            ht = hg * 4 + hx
                            for stc in range(4):
                                for mh in range(2):
                                    last = e.matmul(
                                        psl[stc * 2 + mh],
                                        AR1[:, ht * 512 + stc * 128:
                                            ht * 512 + stc * 128 + 128],
                                        AR5[:, buf * 4096 + hx * 1024 + mh * 512:
                                            buf * 4096 + hx * 1024 + mh * 512 + 512],
                                        start=(ht == 0), stop=False)
                        return last
                    tstep(_mm)
                    w2buf[buf] = P.cnt("s_t")
                phase_ffn._w2buf = w2buf

                def _mmb(e, l=l):
                    last = None
                    for stc in range(4):
                        for mh in range(2):
                            last = e.matmul(psl[stc * 2 + mh],
                                            c_onesf[0:1, 0:128],
                                            sm_b2row[0:1, mh * 512:mh * 512 + 512],
                                            start=False, stop=True)
                    return last
                tt = tstep(_mmb)
                ffht = getattr(phase_ffn, "_ffht", [0, 0])
                for stc in range(4):
                    st = sc_ * 4 + stc
                    ht = htile if stc % 2 == 0 else htile2
                    wait("sync", "s_v", fnT_v)
                    wait("sync", "s_io", ffht[stc % 2])
                    htot = dma("sync", ht, h_dram[st * 128:(st + 1) * 128, :])
                    wait("vector", "s_t", tt)
                    wait("vector", "s_io", htot)
                    av = 0
                    for mh in range(2):
                        av = vstep(lambda e, stc=stc, mh=mh, ht=ht:
                                   e.tensor_add(ht[:, mh * 512:(mh + 1) * 512],
                                                ht[:, mh * 512:(mh + 1) * 512],
                                                psl[stc * 2 + mh]))
                        ffn_ffv[stc * 2 + mh] = av
                    wait("sync", "s_v", av)
                    ffht[stc % 2] = dma("sync",
                                        h_dram[st * 128:(st + 1) * 128, :], ht)
                phase_ffn._ffht = ffht

        def emit_moe():
            wait("gpsimd", "s_t")
            wait("gpsimd", "s_s")
            iot = dma("gpsimd", c_iotac[:, :], iotac_in)
            wait("vector", "s_w", iot)
            wait("sync", "s_io")
            m0_cp = 0
            for st in range(NT):
                wait("sync", "s_v")
                t1 = dma("sync", htile, h_dram[st * 128:(st + 1) * 128, :])
                t2 = dma("sync", xtile, x_in[st * 128:(st + 1) * 128, :])
                wait("vector", "s_io", t2)
                vstep(lambda e: e.tensor_add(htile, htile, xtile))
                wait("vector", "s_v")
                wait("vector", "s_io", m0_cp)
                vt = ln_core(htile, PTB[:, 0:1024], 1e-5)
                wait("sync", "s_v", vt)
                dma("sync", h_dram[st * 128:(st + 1) * 128, :], htile)
                m0_cp = cpair(shr_m1[:, st * D:(st + 1) * D],
                              shr_m1[:, (NT + st) * D:(NT + st + 1) * D],
                              PTB[:, 0:1024])
            flag_write(FL_M1)
            flag_poll(FL_M1)
            wait("sync", "s_t")
            wait("sync", "s_s")
            xn1tot = []
            for stg in range(NTG):
                arr = AR1 if stg < 16 else AR2
                xn1tot.append(dma("sync", arr[:, (stg % 16) * 1024:
                                              (stg % 16) * 1024 + 1024],
                                  shr_m1[:, stg * D:(stg + 1) * D]))

            def xn1t(stg):
                arr = AR1 if stg < 16 else AR2
                return arr[:, (stg % 16) * 1024:(stg % 16) * 1024 + 1024]

            for stg in range(NTG):
                wait("tensor", "s_io", xn1tot[stg])
                for mt in range(NM):
                    ps = ps_a if mt % 2 == 0 else ps_b
                    psv = ps.ap().bitcast(BF16)[:, 0:128]
                    wait("tensor", "s_v")
                    tt = tstep(lambda e, psv=psv, mt=mt, stg=stg:
                               e.transpose(psv, xn1t(stg)[:, mt * 128:(mt + 1) * 128],
                                           c_ident[:, :]))
                    wait("vector", "s_t", tt)
                    vstep(lambda e, psv=psv, mt=mt:
                          e.tensor_copy(sm_xt[:, mt * 128:(mt + 1) * 128], psv))
                wait("tensor", "s_v")

                def _rmm(e):
                    last = None
                    for mt in range(NM):
                        last = e.matmul(ps_ctx.ap()[:, 0:8],
                                        sm_xt[:, mt * 128:(mt + 1) * 128],
                                        c_gatewT[:, mt * 8:(mt + 1) * 8],
                                        start=(mt == 0), stop=(mt == NM - 1))
                    return last
                tt = tstep(_rmm)
                wait("vector", "s_t", tt)
                vstep(lambda e: e.reduce_max(sm_mr[:, 4:5], ps_ctx.ap()[:, 0:8],
                                             axis=AX.X, negate=True))
                wait("scalar", "s_v")
                sstep(lambda e: e.activation(sm_tot[:, 0:8], ps_ctx.ap()[:, 0:8],
                                             AF.Exp, bias=sm_mr[:, 4:5]))
                wait("vector", "s_s")
                vstep(lambda e: e.reduce_sum(sm_mr[:, 5:6], sm_tot[:, 0:8], axis=AX.X))
                wait("vector", "s_v")
                vstep(lambda e: e.reciprocal(sm_mr[:, 6:7], sm_mr[:, 5:6]))
                wait("vector", "s_v")
                vstep(lambda e, stg=stg:
                      e.tensor_scalar(sm_probs[:, stg * 8:stg * 8 + 8],
                                      sm_tot[:, 0:8], sm_mr[:, 6:7], None, AL.mult))
                wait("vector", "s_v")
                vstep(lambda e, stg=stg:
                      e.tensor_sub(sm_tot[:, 8:16], sm_probs[:, stg * 8:stg * 8 + 8],
                                   c_eps[:, :]))
                wait("vector", "s_v")
                vstep(lambda e: e.reduce_max(sm_mr[:, 7:8], sm_tot[:, 8:16], axis=AX.X))
                wait("vector", "s_v")
                vstep(lambda e: e.tensor_scalar(sm_tot[:, 16:24], sm_tot[:, 8:16],
                                                sm_mr[:, 7:8], None, AL.is_ge))
                wait("vector", "s_v")
                vstep(lambda e: e.scalar_tensor_tensor(sm_tot[:, 24:32],
                                                       sm_tot[:, 16:24], -1e9,
                                                       sm_tot[:, 8:16],
                                                       AL.mult, AL.add))
                wait("vector", "s_v")
                vstep(lambda e: e.reduce_max(sm_mr[:, 7:8], sm_tot[:, 24:32],
                                             axis=AX.X))
                wait("vector", "s_v")
                vstep(lambda e: e.tensor_scalar(sm_tot[:, 32:40], sm_tot[:, 24:32],
                                                sm_mr[:, 7:8], None, AL.is_ge))
                wait("vector", "s_v")
                vstep(lambda e, stg=stg:
                      e.tensor_add(sm_g[:, stg * 8:stg * 8 + 8], sm_tot[:, 16:24],
                                   sm_tot[:, 32:40]))
                vstep(lambda e, stg=stg:
                      e.tensor_add(sm_gf[:, stg * 8:stg * 8 + 8], sm_tot[:, 16:24],
                                   sm_tot[:, 32:40]))
                wait("tensor", "s_v")
                tstep(lambda e, stg=stg:
                      e.matmul(ps_b.ap()[:, 0:8], c_ltri[:, :],
                               sm_g[:, stg * 8:stg * 8 + 8], start=True, stop=True))
                tt = tstep(lambda e, stg=stg:
                           e.matmul(ps_bc[0:8, stg:stg + 1],
                                    sm_g[:, stg * 8:stg * 8 + 8],
                                    c_onesb[:, 0:1], start=True, stop=True))
                wait("vector", "s_t", tt)
                vstep(lambda e, stg=stg:
                      e.tensor_copy(sm_cin[:, stg * 8:stg * 8 + 8],
                                    ps_b.ap()[:, 0:8]))
                wait("tensor", "s_v")

            wait("vector", "s_t")
            vstep(lambda e: e.tensor_copy(sm_tot[0:8, 0:32], ps_bc[0:8, 0:32]))
            wait("vector", "s_v")
            vstep(lambda e: e.tensor_copy(sm_tot[0:8, 80:112], sm_tot[0:8, 0:32]))
            cur, oth = 0, 40
            for k in (1, 2, 4, 8, 16):
                wait("vector", "s_v")
                vstep(lambda e, cur=cur, oth=oth:
                      e.tensor_copy(sm_tot[0:8, oth:oth + 32],
                                    sm_tot[0:8, cur:cur + 32]))
                wait("vector", "s_v")
                vstep(lambda e, cur=cur, oth=oth, k=k:
                      e.tensor_add(sm_tot[0:8, oth + k:oth + 32],
                                   sm_tot[0:8, cur + k:cur + 32],
                                   sm_tot[0:8, cur:cur + 32 - k]))
                cur, oth = oth, cur
            wait("vector", "s_v")
            vstep(lambda e, cur=cur:
                  e.tensor_sub(sm_tot[0:8, 112:144], sm_tot[0:8, cur:cur + 32],
                               sm_tot[0:8, 80:112]))
            exv = P.cnt("s_v")
            wait("sync", "s_v", exv)
            erview = sm_erow.rearrange("o (t e) -> o t e", e=8)
            for e_ in range(8):
                dma("sync", erview[:, :, e_], sm_tot[e_:e_ + 1, 112:144], slow=True)
            er_tot = P.cnt("s_io")

            for stg in range(NTG):
                wait("tensor", "s_io", er_tot)
                wait("tensor", "s_v")
                tt = tstep(lambda e, stg=stg:
                           e.matmul(ps_a.ap()[:, 0:8], c_onesf[0:1, 0:128],
                                    sm_erow[0:1, stg * 8:stg * 8 + 8],
                                    start=True, stop=True))
                wait("vector", "s_t", tt)
                vstep(lambda e, stg=stg:
                      e.tensor_add(sm_tot[:, 0:8], sm_cin[:, stg * 8:stg * 8 + 8],
                                   ps_a.ap()[:, 0:8]))
                wait("vector", "s_v")
                vstep(lambda e, stg=stg:
                      e.scalar_tensor_tensor(sm_tot[:, 8:16], sm_tot[:, 0:8],
                                             CAP + 0.5, sm_gf[:, stg * 8:stg * 8 + 8],
                                             AL.is_le, AL.mult))
                wait("vector", "s_v")
                vstep(lambda e, stg=stg:
                      e.tensor_mul(sm_wcmb[:, stg * 8:stg * 8 + 8], sm_tot[:, 8:16],
                                   sm_probs[:, stg * 8:stg * 8 + 8]))
                vstep(lambda e, stg=stg:
                      e.tensor_mul(sm_posm[:, stg * 8:stg * 8 + 8], sm_tot[:, 0:8],
                                   sm_tot[:, 8:16]))
                wait("vector", "s_v")

            CCH = [(0, 512), (512, 512), (1024, 256)]
            for ex in range(4):
                for cb, cw in CCH:
                    ncp = cw // 128
                    for cp in range(ncp):
                        cptg = (cb // 128) + cp
                        wait("tensor", "s_v")
                        mk_mm = [P.cnt("s_t")] * 2
                        mk_bld = [P.cnt("s_v")] * 2
                        for st in range(NTG):
                            mk = sm_mask if st % 2 == 0 else sm_mask2
                            wait("vector", "s_t", mk_mm[st % 2])
                            mk_bld[st % 2] = vstep(
                                lambda e, mk=mk, cptg=cptg, st=st, ex=ex:
                                e.tensor_scalar(mk[:, :],
                                                c_iotac[:, cptg * 128:
                                                        (cptg + 1) * 128],
                                                sm_posm[:, st * 8 + ex:
                                                        st * 8 + ex + 1],
                                                None, AL.is_equal))
                            wait("tensor", "s_v", mk_bld[st % 2])

                            def _mm(e, mk=mk, st=st, first=(st == 0),
                                    last=(st == NTG - 1)):
                                e.matmul(ps_a.ap()[:, :], mk[:, :],
                                         xn1t(st)[:, 0:512],
                                         start=first, stop=last)
                                return e.matmul(ps_b.ap()[:, :], mk[:, :],
                                                xn1t(st)[:, 512:1024],
                                                start=first, stop=last)
                            mk_mm[st % 2] = tstep(_mm)
                        wait("scalar", "s_t")
                        sstep(lambda e, cp=cp:
                              e.copy(AR3[:, cp * 1024:cp * 1024 + 512],
                                     ps_a.ap()[:, :]))
                        sstep(lambda e, cp=cp:
                              e.copy(AR3[:, cp * 1024 + 512:cp * 1024 + 1024],
                                     ps_b.ap()[:, :]))
                        wait("tensor", "s_s")
                    for cp in range(ncp):
                        for mt in range(NM):
                            ps = ps_a if (cp * NM + mt) % 2 == 0 else ps_b
                            psv = ps.ap().bitcast(BF16)[:, 0:128]
                            wait("tensor", "s_v")
                            tt = tstep(lambda e, psv=psv, cp=cp, mt=mt:
                                       e.transpose(psv,
                                                   AR3[cp * 128:(cp + 1) * 128,
                                                       0:0]
                                                   if False else
                                                   AR3[:, cp * 1024 + mt * 128:
                                                       cp * 1024 + mt * 128 + 128],
                                                   c_ident[:, :]))
                            wait("vector", "s_t", tt)
                            vstep(lambda e, psv=psv, cp=cp, mt=mt:
                                  e.tensor_copy(AR3[:, 4096 + mt * 512 + cp * 128:
                                                    4096 + mt * 512 + cp * 128 + 128],
                                                psv))
                    wait("vector", "s_t")
                    w1v = mw1_in[ex].rearrange("(mt p) (hg x) -> p hg mt x",
                                               p=128, x=128)
                    w3v = mw3_in[ex].rearrange("(mt p) (hg x) -> p hg mt x",
                                               p=128, x=128)
                    for hg in range(14):
                        wait("gpsimd", "s_t")
                        wait("gpsimd", "s_v")
                        for hx in range(2):
                            dma("gpsimd",
                                AR7.ap()[:, hx * 1024:(hx + 1) * 1024].rearrange(
                                    "p (mt x) -> p mt x", x=128),
                                w1v[:, hg * 2 + hx])
                            dma("gpsimd",
                                AR7.ap()[:, 2048 + hx * 1024:2048 + (hx + 1) * 1024]
                                .rearrange("p (mt x) -> p mt x", x=128),
                                w3v[:, hg * 2 + hx])
                        wt3 = P.cnt("s_w")
                        wait("tensor", "s_w", wt3)
                        for hx in range(2):
                            hc = hg * 2 + hx
                            wait("tensor", "s_v")

                            def _mm(e, hx=hx, cw=cw):
                                last = None
                                for mt in range(NM):
                                    e.matmul(ps_s0.ap()[:, 0:cw],
                                             AR7[:, hx * 1024 + mt * 128:
                                                 hx * 1024 + mt * 128 + 128],
                                             AR3[:, 4096 + mt * 512:
                                                 4096 + mt * 512 + cw],
                                             start=(mt == 0), stop=(mt == NM - 1))
                                for mt in range(NM):
                                    last = e.matmul(
                                        ps_s1.ap()[:, 0:cw],
                                        AR7[:, 2048 + hx * 1024 + mt * 128:
                                            2048 + hx * 1024 + mt * 128 + 128],
                                        AR3[:, 4096 + mt * 512:
                                            4096 + mt * 512 + cw],
                                        start=(mt == 0), stop=(mt == NM - 1))
                                return last
                            tt = tstep(_mm)
                            wait("scalar", "s_t", tt)
                            st_ = sstep(lambda e, cw=cw:
                                        e.activation(silu_sc[:, 0:cw],
                                                     ps_s0.ap()[:, 0:cw], AF.Silu))
                            wait("vector", "s_s", st_)
                            vstep(lambda e, hc=hc, cw=cw:
                                  e.tensor_mul(AR4[:, hc * 512:hc * 512 + cw],
                                               silu_sc[:, 0:cw],
                                               ps_s1.ap()[:, 0:cw]))
                    wait("tensor", "s_v")
                    w2v = mw2_in[ex].rearrange("(hg p) m -> p hg m", p=128)
                    for hg in range(7):
                        buf = hg % 2
                        wait("gpsimd", "s_t")
                        wtot = dma("gpsimd",
                                   AR5.ap().rearrange("p (b c m) -> p b c m",
                                                      b=2, c=4)[:, buf],
                                   w2v[:, hg * 4:(hg + 1) * 4])
                        wait("tensor", "s_w", wtot)

                        def _mm(e, hg=hg, buf=buf, ncp=ncp, cw=cw):
                            last = None
                            for hx in range(4):
                                hc = hg * 4 + hx
                                for cp in range(ncp):
                                    cpw = min(128, cw - cp * 128)
                                    for mh in range(2):
                                        last = e.matmul(
                                            psl8[cp * 2 + mh],
                                            AR4[:, hc * 512 + cp * 128:
                                                hc * 512 + cp * 128 + cpw],
                                            AR5[:, buf * 4096 + hx * 1024 +
                                                mh * 512:buf * 4096 + hx * 1024 +
                                                mh * 512 + 512],
                                            start=(hc == 0), stop=(hc == NHC - 1))
                            return last
                        tstep(_mm)
                    wait("scalar", "s_t")
                    for cp in range(ncp):
                        for mh in range(2):
                            sstep(lambda e, cp=cp, mh=mh:
                                  e.copy(AR3[:, 8192 + cp * 1024 + mh * 512:
                                             8192 + cp * 1024 + mh * 512 + 512],
                                         psl8[cp * 2 + mh]))
                    stot = P.cnt("s_s")
                    wait("sync", "s_s", stot)
                    for cp in range(ncp):
                        cptg = (cb // 128) + cp
                        dma("sync",
                            eo_dram[ex * CAP + cptg * 128:
                                    ex * CAP + cptg * 128 + 128, :],
                            AR3[:, 8192 + cp * 1024:8192 + cp * 1024 + 1024])
                    wait("tensor", "s_io")

            wait("sync", "s_t")
            eot = []
            for k in range(40):
                arr, base = ((AR1, k) if k < 16 else
                             (AR2, k - 16) if k < 32 else (AR5, k - 32))
                eot.append(dma("sync", arr[:, base * 1024:(base + 1) * 1024],
                               eo_dram[k * 128:(k + 1) * 128, :]))

            def eo_tile(k):
                arr, base = ((AR1, k) if k < 16 else
                             (AR2, k - 16) if k < 32 else (AR5, k - 32))
                return arr[:, base * 1024:(base + 1) * 1024]

            AR4f = AR4.ap().bitcast(F32)
            cm_mm = [P.cnt("s_t")] * 2
            tr_ps = [P.cnt("s_v")] * 2
            mk_b = [P.cnt("s_t")] * 2
            for stg in range(NTG):
                wait("tensor", "s_io", eot[-1])
                cbuf = (stg % 2) * 5120
                for ex in range(4):
                    for cp in range(10):
                        k = ex * 10 + cp
                        mk = sm_mask if k % 2 == 0 else sm_mask2
                        wait("vector", "s_t", mk_b[k % 2])
                        bld = vstep(lambda e, ex=ex, cp=cp, stg=stg, mk=mk:
                                    e.tensor_scalar(
                                        mk[:, :],
                                        c_iotac[:, cp * 128:(cp + 1) * 128],
                                        sm_posm[:, stg * 8 + ex:stg * 8 + ex + 1],
                                        sm_wcmb[:, stg * 8 + ex:stg * 8 + ex + 1],
                                        AL.is_equal, AL.mult))
                        ps = ps_a if k % 2 == 0 else ps_b
                        psv = ps.ap().bitcast(BF16)[:, 0:128]
                        wait("tensor", "s_v", max(bld, tr_ps[k % 2]))
                        mk_b[k % 2] = tstep(lambda e, mk=mk, psv=psv:
                                            e.transpose(psv, mk[:, :],
                                                        c_ident[:, :]))
                        wait("vector", "s_t", mk_b[k % 2])
                        wait("vector", "s_t", cm_mm[stg % 2])
                        tr_ps[k % 2] = vstep(
                            lambda e, k=k, psv=psv, cbuf=cbuf:
                            e.tensor_copy(AR4[:, cbuf + k * 128:
                                              cbuf + k * 128 + 128], psv))
                cmv = P.cnt("s_v")
                m6cp = getattr(emit_moe, "_m6cp", 0)
                for mh in range(2):
                    ps = ps_ctx if mh == 0 else ps_bc
                    wait("tensor", "s_v", cmv)
                    wait("tensor", "s_s")
                    wait("scalar", "s_io", m6cp)

                    def _mm(e, mh=mh, ps=ps, cbuf=cbuf):
                        last = None
                        for k in range(40):
                            last = e.matmul(ps.ap()[:, :],
                                            AR4[:, cbuf + k * 128:
                                                cbuf + k * 128 + 128],
                                            eo_tile(k)[:, mh * 512:mh * 512 + 512],
                                            start=(k == 0), stop=(k == 39))
                        return last
                    tt = tstep(_mm)
                    cm_mm[stg % 2] = tt
                    wait("scalar", "s_t", tt)
                    sstep(lambda e, mh=mh, ps=ps:
                          e.copy(AR4f[:, 5120 + mh * 512:5120 + mh * 512 + 512],
                                 ps.ap()[:, :]))
                stot = P.cnt("s_s")
                wait("sync", "s_s", stot)
                emit_moe._m6cp = cpair(
                    shr_m2[:, stg * D:(stg + 1) * D],
                    shr_m2[:, (NTG + stg) * D:(NTG + stg + 1) * D],
                    AR4f[:, 5120:6144])
            flag_write(FL_M2)
            flag_poll(FL_M2)
            for st in range(NT):
                wait("sync", "s_v")
                t1 = dma("sync", htile, h_dram[st * 128:(st + 1) * 128, :])
                pp0 = AR4f[:, 6144:7168]
                pp1 = AR4f[:, 7168:8192]
                t2 = cpair(pp0, pp0,
                           shr_m2[:, st * D:(st + 1) * D],
                           shr_m2[:, (NT + st) * D:(NT + st + 1) * D])
                t3 = cpair(pp1, pp1,
                           shr_m2[:, (NTG + st) * D:(NTG + st + 1) * D],
                           shr_m2[:, (NTG + NT + st) * D:(NTG + NT + st + 1) * D])
                wait("vector", "s_io", t3)
                vstep(lambda e, pp0=pp0: e.tensor_add(htile, htile, pp0))
                wait("vector", "s_v")
                vstep(lambda e, pp1=pp1: e.tensor_add(htile, htile, pp1))
                vt = P.cnt("s_v")
                wait("sync", "s_v", vt)
                dma("sync", y_out[st * 128:(st + 1) * 128, :], htile)
            wait("sync", "s_io")

        psl8 = [ps_s0.ap()[:, 0:512], ps_s0.ap()[:, 512:1024],
                ps_s1.ap()[:, 0:512], ps_s1.ap()[:, 512:1024],
                ps_ctx.ap()[:, :], ps_bc.ap()[:, :],
                ps_a.ap()[:, :], ps_b.ap()[:, :]]

        for l in range(_NLB):
            if _STAGE == "ln0":
                break
            wait("sync", "s_t")
            dma("sync", sm_bvrow[0:1, :], bvrow_in[0:1, l * 512:(l + 1) * 512])
            dma("sync", sm_borow[0:1, :], borow_in[0:1, l * D:(l + 1) * D])
            bias_tot = dma("sync", sm_b2row[0:1, :], b2row_in[0:1, l * D:(l + 1) * D])
            wait("tensor", "s_io", bias_tot)
            ln_pass(l, 1, AR1)
            vt = P.cnt("s_v")
            wait("sync", "s_v", vt)
            sxn = shr_xn[l].rearrange("p (mt s) -> p mt s", mt=NM)
            src1 = AR1.ap().rearrange("p (mt s) -> p mt s", mt=NM)
            cpair(sxn[:, :, 0:SH], sxn[:, :, SH:S], src1)
            flag_write(2 * l)
            phase_qkv(l)
            phase_attn(l)
            if _STAGE == "ctx0":
                wait("sync", "s_v")
                t = dma("sync", dbg_out, AR1.ap())
                wait("sync", "s_io", t)
                break
            phase_wo(l)
            if _STAGE == "mid0":
                break
            phase_ffn(l)

        if _STAGE in ("ln0", "mid0", "l0", "ctx0", "attn4"):
            wait("sync", "s_v")
            wait("sync", "s_io")
            ytot = dma("sync", y_out, h_dram[:, :])
            wait("sync", "s_io", ytot)
        else:
            emit_moe()

        @block.sync
        def _(eng):
            P.play("sync", eng)

        @block.gpsimd
        def _(eng):
            P.play("gpsimd", eng)

        @block.vector
        def _(eng):
            P.play("vector", eng)

        @block.scalar
        def _(eng):
            P.play("scalar", eng)

        @block.tensor
        def _(eng):
            P.play("tensor", eng)

    return nc


def _host_consts():
    import ml_dtypes
    bf = ml_dtypes.bfloat16
    inv = (1.0 / 10000.0 ** (np.arange(0, DH, 2, dtype=np.float64) / DH))
    pos = np.arange(S, dtype=np.float64)
    ang = pos[None, :] * inv[:, None]
    sin_t = np.sin(ang).astype(np.float32)
    cos_t = np.cos(ang).astype(np.float32)
    c = {}
    c["trig"] = np.concatenate([sin_t, cos_t, cos_t, sin_t],
                               axis=0).astype(bf)
    k_i = np.arange(128)
    c["tri"] = (k_i[:, None] <= k_i[None, :]).astype(bf)
    c["ltri"] = (k_i[:, None] <= k_i[None, :]).astype(bf)
    c["identb"] = np.eye(128, dtype=bf)
    c["identf"] = np.eye(128, dtype=np.float32)
    c["onesb"] = np.ones((128, 128), dtype=bf)
    c["onesf"] = np.ones((1, 512), dtype=np.float32)
    c["iotac"] = np.tile(np.arange(1, CAP + 1, dtype=np.float32)[None, :],
                         (128, 1))
    c["epsiota"] = np.tile((np.arange(E, dtype=np.float32) * 1e-10)[None, :],
                           (128, 1))
    return c


def kernel(L0_attn_x, L0_attn_Wq, L0_attn_bq, L0_attn_Wk, L0_attn_bk,
           L0_attn_Wv, L0_attn_bv, L0_attn_Wo, L0_attn_bo, L0_attn_u_bias,
           L0_attn_W1, L0_attn_b1, L0_attn_W2, L0_attn_b2, L0_attn_ln1_w,
           L0_attn_ln1_b, L0_attn_ln2_w, L0_attn_ln2_b, L0_ffn_gate_w,
           L0_ffn_w1, L0_ffn_w2, L0_ffn_w3):
    import ml_dtypes
    bf = ml_dtypes.bfloat16
    f32 = np.float32
    ascon = np.ascontiguousarray

    if "nc" not in _CACHE:
        _CACHE["nc"] = build_nc()
        _CACHE["consts"] = _host_consts()
    nc = _CACHE["nc"]
    consts = _CACHE["consts"]
    _CACHE["epoch"] = _CACHE.get("epoch", np.random.randint(1, 2 ** 30)) + 1
    epoch = np.full((1, 16), (_CACHE["epoch"] * 2654435761 + 12345) & 0x7FFFFFFF,
                    dtype=np.int32)

    x = np.asarray(L0_attn_x, f32)
    Wq = np.asarray(L0_attn_Wq, f32)
    Wk = np.asarray(L0_attn_Wk, f32)
    Wv = np.asarray(L0_attn_Wv, f32)
    Wo = np.asarray(L0_attn_Wo, f32)
    bq = np.asarray(L0_attn_bq, f32)
    bk = np.asarray(L0_attn_bk, f32)
    bv = np.asarray(L0_attn_bv, f32)
    bo = np.asarray(L0_attn_bo, f32)
    ub = np.asarray(L0_attn_u_bias, f32)
    W1 = np.asarray(L0_attn_W1, f32)
    b1 = np.asarray(L0_attn_b1, f32)
    W2 = np.asarray(L0_attn_W2, f32)
    b2 = np.asarray(L0_attn_b2, f32)
    l1w = np.asarray(L0_attn_ln1_w, f32)
    l1b = np.asarray(L0_attn_ln1_b, f32)
    l2w = np.asarray(L0_attn_ln2_w, f32)
    l2b = np.asarray(L0_attn_ln2_b, f32)
    gw = np.asarray(L0_ffn_gate_w, f32)
    mw1 = np.asarray(L0_ffn_w1, f32)
    mw2 = np.asarray(L0_ffn_w2, f32)
    mw3 = np.asarray(L0_ffn_w3, f32)

    prep = {}
    for hh in range(2):
        hs = slice(512 * hh, 512 * hh + 512)
        p = {}
        p["wqT"] = ascon(Wq[:, hs, :].transpose(0, 2, 1))
        p["wkT"] = ascon(Wk[:, hs, :].transpose(0, 2, 1))
        p["wvT"] = ascon(Wv[:, hs, :].transpose(0, 2, 1))
        p["woTm"] = ascon(Wo[:, :, hs].transpose(0, 2, 1))
        ps_ = slice(512 * (1 - hh), 512 * (1 - hh) + 512)
        p["woTp"] = ascon(Wo[:, :, ps_].transpose(0, 2, 1))
        bqkv = np.zeros((NL, 1536), f32)
        for l in range(NL):
            for wi, bb in enumerate((bq, bk, bv)):
                bqkv[l, wi * 512:(wi + 1) * 512] = bb[l, 512 * hh:512 * (hh + 1)]
        p["bqkv"] = bqkv
        uT = np.zeros((64, NL * 8), f32)
        for l in range(NL):
            for h in range(8):
                uT[:, l * 8 + h] = ub[l, 8 * hh + h, :]
        p["uT"] = uT
        eperm = list(range(4 * hh, 4 * hh + 4)) + list(range(4 * (1 - hh),
                                                             4 * (1 - hh) + 4))
        gperm = gw[eperm, :]
        gT = ascon(gperm.T)
        gtile = np.zeros((128, NM * E), f32)
        for mt in range(NM):
            gtile[:, mt * E:(mt + 1) * E] = gT[mt * 128:(mt + 1) * 128, :]
        p["gatewT"] = gtile.astype(bf)
        p["mw1"] = ascon(mw1[eperm[:4]])
        p["mw3"] = ascon(mw3[eperm[:4]])
        p["mw2"] = ascon(mw2[eperm[:4]])
        prep[hh] = p

    w1T = ascon(W1.transpose(0, 2, 1))
    w2T = ascon(W2.transpose(0, 2, 1))
    b1T = np.zeros((128, NL * 32), f32)
    for l in range(NL):
        for hc in range(32):
            b1T[:, l * 32 + hc] = b1[l, hc * 128:(hc + 1) * 128]
    lncols = np.zeros((128, NL * 4 * NM), f32)
    for l in range(NL):
        for wi, vv in enumerate((l1w, l1b, l2w, l2b)):
            for mt in range(NM):
                lncols[:, (l * 4 + wi) * NM + mt] = vv[l, mt * 128:(mt + 1) * 128]
    borow = bo.reshape(1, NL * D).astype(f32)
    b2row = b2.reshape(1, NL * D).astype(f32)
    bvrow = np.zeros((1, NL * 512), f32)

    in_maps = []
    for c in range(8):
        b_, hh = divmod(c, 2)
        p = prep[hh]
        m = {
            "x_my": ascon(x[b_, SH * hh:SH * (hh + 1), :]),
            "epoch": epoch,
            "wqT": p["wqT"], "wkT": p["wkT"], "wvT": p["wvT"],
            "woTm": p["woTm"], "woTp": p["woTp"],
            "w1T": w1T, "w2T": w2T,
            "bqkv": p["bqkv"], "uT": p["uT"], "b1T": b1T, "lncols": lncols,
            "borow": borow, "b2row": b2row,
            "bvrow": ascon(bv[:, 512 * hh:512 * (hh + 1)].reshape(1, NL * 512)),
            "cvals": np.tile(np.array([1e-5, 1e-6, -0.5, 0.125, -12.0, -1e9,
                                       CAP + 0.5, 0.0], np.float32)[None, :],
                             (128, 1)),
            "trig": consts["trig"],
            "tri": consts["tri"], "ltri": consts["ltri"],
            "identb": consts["identb"], "identf": consts["identf"],
            "onesb": consts["onesb"], "onesf": consts["onesf"],
            "iotac": consts["iotac"],
            "epsiota": consts["epsiota"],
            "gatewT": p["gatewT"],
            "mw1": p["mw1"], "mw3": p["mw3"], "mw2": p["mw2"],
        }
        in_maps.append(m)

    res = run_bass_kernel_spmd(nc, in_maps, core_ids=list(range(8)))
    if _STAGE == "ctx0":
        _CACHE["dbg"] = [np.asarray(res.results[c]["dbg"]) for c in range(8)]
    out = np.empty((B, S, D), f32)
    for c in range(8):
        b_, hh = divmod(c, 2)
        out[b_, SH * hh:SH * (hh + 1), :] = res.results[c]["y"]
    return out
